# version 03 (final)
# baseline (speedup 1.0000x reference)
import numpy as np
from contextlib import ExitStack

import concourse.bass as bass
import concourse.mybir as mybir
from concourse.bass_utils import run_bass_kernel_spmd

F32 = mybir.dt.float32
BF16 = mybir.dt.bfloat16
I32 = mybir.dt.int32
AF = mybir.ActivationFunctionType
ALU = mybir.AluOpType
AX = mybir.AxisListType

B = 2
S = 16384
M = 4096
D = 1024
KS = 4096
NH = 16
DH = 64
HL = 4
EPS_LN = 1e-5
EPS_ATTN = 1e-6
FF0 = 3072
FF1 = 4096
E = 8
FMOE = 4096



class SC:
    __slots__ = ("h", "n", "name")

    def __init__(self, h, name):
        self.h = h
        self.n = 0
        self.name = name


def mk_sems(nc, ctx, pfx, names):
    return {nm: SC(ctx.enter_context(nc.semaphore(f"{pfx}_{nm}")), f"{pfx}_{nm}")
            for nm in names}


class _InstStub:
    __slots__ = ()

    def then_inc(self, *a, **k):
        return self


_INST_STUB = _InstStub()


class _EngStub:
    __slots__ = ()

    def __getattr__(self, name):
        def f(*a, **k):
            return _INST_STUB
        return f


_ENG_STUB = _EngStub()

_PASSTHRU = {"wait_ge", "sem_inc", "dma_start", "indirect_dma_start",
             "dma_start_transpose", "load_library", "collective_compute",
             "trigger_dma"}


class EngProxy:

    def __init__(self, env, raw, sc, serialize):
        self._env = env
        self._raw = raw
        self._sc = sc
        self._ser = serialize

    def __getattr__(self, attr):
        raw = self._raw
        real = getattr(raw, attr)
        if attr in _PASSTHRU:
            return real
        sc = self._sc
        ser = self._ser

        def wrapped(*a, **k):
            if ser and sc.n > 0:
                raw.wait_ge(sc.h, sc.n)
            instr = real(*a, **k)
            instr.then_inc(sc.h, 1)
            sc.n += 1
            return instr
        return wrapped


class Env:
    def __init__(self):
        self.dry = True
        self.marks = {}

    def inc(self, instr, sc, k=16):
        if not self.dry:
            instr.then_inc(sc.h, k)
        sc.n += k

    def mark(self, name, sc):
        if self.dry:
            assert name not in self.marks, f"duplicate mark {name}"
            self.marks[name] = (sc, sc.n)
        else:
            got = self.marks.get(name)
            assert got is not None and got[0] is sc and got[1] == sc.n, \
                f"mark mismatch {name}: {got} vs {(sc, sc.n)}"

    def pub(self, proxy, name):
        sc = proxy._sc
        if not self.dry:
            proxy._raw.wait_ge(sc.h, sc.n)
            proxy._raw.sem_inc(sc.h, 1)
        sc.n += 1
        self.mark(name, sc)

    def wait(self, eng, name):
        if self.dry:
            return
        sc, n = self.marks[name]
        raw = eng._raw if isinstance(eng, EngProxy) else eng
        raw.wait_ge(sc.h, n)


def run_stage(nc, sems, stage_fns):
    serial = {"vector": True, "scalar": True, "gpsimd": True, "tensor": False,
              "sync": False}
    autosc = {"vector": "v", "scalar": "s", "gpsimd": "g", "tensor": "t",
              "sync": None}
    snap = [(sc, sc.n) for sc in sems.values()]
    env = Env()
    env.dry = True
    for name, fn in stage_fns:
        key = autosc[name]
        if key is None:
            fn(env, _ENG_STUB)
        else:
            fn(env, EngProxy(env, _ENG_STUB, sems[key], serial[name]))
    finals = [(sc, sc.n) for sc in sems.values()]
    for sc, n in snap:
        sc.n = n
    env.dry = False
    with nc.Block() as block:
        for name, fn in stage_fns:
            key = autosc[name]

            def body(eng, fn=fn, key=key, name=name):
                if key is None:
                    fn(env, eng)
                else:
                    fn(env, EngProxy(env, eng, sems[key], serial[name]))
            getattr(block, name)(body)

        @block.sync
        def _(sync):
            for sc, n in finals:
                if n > 0:
                    sync.wait_ge(sc.h, n)
    for sc, n in finals:
        assert sc.n == n, f"pass divergence on {sc.name}: {sc.n} vs {n}"
    nc.all_engine_barrier()


def make_identity(g, identity):
    sq = identity.shape[0]
    g.memset(identity, 0.0)
    g.affine_select(
        out=identity, in_=identity,
        compare_op=ALU.not_equal,
        fill=1.0, base=0,
        pattern=[[-1, sq]],
        channel_multiplier=1,
    )



def ln_vector(env, v, pfx, i, x_ap, ws, out=None, out2=None):
    inv = 1.0 / x_ap.shape[-1]
    v.reduce_sum(out=ws["mu"][:], in_=x_ap, axis=AX.X)
    v.tensor_scalar_mul(out=ws["mu"][:], in0=ws["mu"][:], scalar1=inv)
    v.tensor_mul(out=ws["mu2"][:], in0=ws["mu"][:], in1=ws["mu"][:])
    env.pub(v, f"{pfx}mu{i}")
    env.wait(v, f"{pfx}ssq{i}")
    v.tensor_scalar_mul(out=ws["var"][:], in0=ws["ssq"][:], scalar1=inv)
    v.tensor_sub(out=ws["var"][:], in0=ws["var"][:], in1=ws["mu2"][:])
    env.pub(v, f"{pfx}var{i}")
    env.wait(v, f"{pfx}rstd{i}")
    if out is not None:
        v.tensor_scalar(out=out, in0=x_ap, scalar1=ws["mu"][:],
                        scalar2=ws["rstd"][:], op0=ALU.subtract, op1=ALU.mult)
    if out2 is not None:
        v.tensor_scalar(out=out2, in0=x_ap, scalar1=ws["mu"][:],
                        scalar2=ws["rstd"][:], op0=ALU.subtract, op1=ALU.mult)
    env.pub(v, f"{pfx}ln{i}")


def ln_scalar(env, s, pfx, i, x_ap, ws):
    s.activation(ws["sqscr"][:], x_ap, AF.Square, accum_out=ws["ssq"][:])
    env.pub(s, f"{pfx}ssq{i}")
    env.wait(s, f"{pfx}var{i}")
    s.activation(ws["lnv"][:], ws["var"][:], AF.Ln, bias=ws["eps"][:])
    s.activation(ws["rstd"][:], ws["lnv"][:], AF.Exp, scale=-0.5)
    env.pub(s, f"{pfx}rstd{i}")



def emit_l0_attn(nc, ctx, xfull, xown, keys, values, wqT, wkT, wvT,
                 bq, bk, bv, x1out, debug_outs=None):
    sb = lambda n, shp, dt: ctx.enter_context(nc.sbuf_tensor(n, shp, dt))
    sems = mk_sems(nc, ctx, "la", ["ld", "st", "v", "s", "t", "g"])
    ld, sst = sems["ld"], sems["st"]
    sv, ss, st_, sg = sems["v"], sems["s"], sems["t"], sems["g"]

    r_scr = nc.dram_tensor("la_r_scr", [M, D], F32, kind="Internal").ap()
    a_scr_t = nc.dram_tensor("la_a_scr", [HL * DH, S], F32, kind="Internal")
    a_scr = a_scr_t.ap()

    ident = sb("la_ident", [128, 128], BF16)
    ones64 = sb("la_ones64", [64, 1], BF16)
    ones1f = sb("la_ones1f", [1, 128], F32)
    wq_bf = sb("la_wq", [128, 8 * 256], BF16)
    wk_bf = sb("la_wk", [128, 8 * 256], BF16)
    wv_bf = sb("la_wv", [128, 8 * 256], BF16)
    bq_sb = sb("la_bq", [128, 2], F32)
    bk_sb = sb("la_bk", [128, 2], F32)
    bv_sb = sb("la_bv", [128, 2], F32)
    k_fm = sb("la_kfm", [128, 2 * KS], BF16)
    v_aug = sb("la_vaug", [128, 32 * HL * 65], BF16)
    kmax2 = sb("la_kmax2", [1, HL], F32)

    wstg = [sb(f"la_wstg{i}", [128, 256], F32) for i in range(2)]
    xstg = [sb(f"la_xstg{i}", [128, D], F32) for i in range(2)]
    xbf = [sb(f"la_xbf{i}", [128, D], BF16) for i in range(2)]
    lnT = sb("la_lnT", [128, 8 * 512], BF16)

    ws = {
        "sqscr": sb("la_sqscr", [128, D], F32),
        "mu": sb("la_mu", [128, 1], F32),
        "mu2": sb("la_mu2", [128, 1], F32),
        "ssq": sb("la_ssq", [128, 1], F32),
        "var": sb("la_var", [128, 1], F32),
        "lnv": sb("la_lnv", [128, 1], F32),
        "rstd": sb("la_rstd", [128, 1], F32),
        "eps": sb("la_eps", [128, 1], F32),
        "sv": sv, "ss": ss,
    }
    xln32 = sb("la_xln32", [128, D], F32)
    r32 = sb("la_r32", [128, D], F32)

    sq_q = sb("la_sqq", [64, 512], BF16)
    qm2 = sb("la_qm2", [1, 1], F32)
    b2t = sb("la_b2t", [1, 1], F32)
    lc = sb("la_lc", [1, 1], F32)
    lc2 = sb("la_lc2", [1, 1], F32)
    negc1 = sb("la_negc1", [1, 1], F32)
    negc_sb = sb("la_negc", [128, 1], F32)
    zrow = sb("la_zrow", [1, 512], F32)
    av_sb = sb("la_avsb", [64, 512], F32)
    nrm_bf = [sb(f"la_nrm{i}", [64, 512], F32) for i in range(4)]
    p_bf = [sb(f"la_p{i}", [128, 1024], BF16) for i in range(2)]


    ctx1 = ExitStack()
    kv_fm = ctx1.enter_context(nc.sbuf_tensor("la_kvfm", [128, 8 * KS], BF16))
    v_pfm = ctx1.enter_context(nc.sbuf_tensor("la_vpfm", [128, 2 * KS], BF16))

    def s0_gpsimd(env, g):
        make_identity(g, ident[:])
        g.memset(ones64[:], 1.0)
        g.memset(ones1f[:], 1.0)
        g.memset(v_aug[:], 1.0)
        g.memset(ws["eps"][:], EPS_LN)
        g.memset(kmax2[:], 0.0)

    def s0_sync(env, sync):
        for half in range(2):
            for bsrc, bdst in ((bq, bq_sb), (bk, bk_sb), (bv, bv_sb)):
                env.inc(sync.dma_start(out=bdst[:, half:half + 1],
                                       in_=bsrc[half * 128:(half + 1) * 128, 0:1]),
                        ld)
        for mi, w in enumerate([wqT, wkT, wvT]):
            for ic in range(8):
                kk = mi * 8 + ic
                if kk >= 2:
                    env.wait(sync, f"wc{kk - 2}")
                env.inc(sync.dma_start(out=wstg[kk % 2][:],
                                       in_=w[ic * 128:(ic + 1) * 128, :]), ld)
                env.mark(f"wl{kk}", ld)

    def s0_vector(env, v):
        for mi, wdst in enumerate([wq_bf, wk_bf, wv_bf]):
            for ic in range(8):
                kk = mi * 8 + ic
                env.wait(v, f"wl{kk}")
                v.tensor_copy(out=wdst[:, ic * 256:(ic + 1) * 256],
                              in_=wstg[kk % 2][:])
                env.pub(v, f"wc{kk}")

    run_stage(nc, sems,
              [("gpsimd", s0_gpsimd), ("sync", s0_sync), ("vector", s0_vector)])

    def stage_transpose_in(tag, src_dram, ntiles):
        pctx = ExitStack()
        pt = [pctx.enter_context(nc.psum_tensor(f"{tag}_pt{i}", [128, 128], BF16))
              for i in range(2)]

        def f_sync(env, sync):
            for i in range(ntiles):
                if i >= 2:
                    env.wait(sync, f"{tag}cast{i - 2}")
                env.inc(sync.dma_start(out=xstg[i % 2][:],
                                       in_=src_dram[i * 128:(i + 1) * 128, :]), ld)
                env.mark(f"{tag}ld{i}", ld)

        def f_vector(env, v):
            for i in range(ntiles):
                env.wait(v, f"{tag}ld{i}")
                if i >= 2:
                    env.wait(v, f"{tag}tr{i - 2}_7")
                v.tensor_copy(out=xbf[i % 2][:], in_=xstg[i % 2][:])
                env.pub(v, f"{tag}cast{i}")

        def f_tensor(env, t):
            for i in range(ntiles):
                env.wait(t, f"{tag}cast{i}")
                for jc in range(8):
                    g = i * 8 + jc
                    if g >= 2:
                        env.wait(t, f"{tag}ev{g - 2}")
                    t.transpose(pt[g % 2][:],
                                xbf[i % 2][:, jc * 128:(jc + 1) * 128],
                                ident[:])
                    env.mark(f"{tag}tr{i}_{jc}", st_)

        def f_scalar(env, s):
            for i in range(ntiles):
                for jc in range(8):
                    g = i * 8 + jc
                    env.wait(s, f"{tag}tr{i}_{jc}")
                    s.copy(kv_fm[:, jc * KS + i * 128:jc * KS + (i + 1) * 128],
                           pt[g % 2][:])
                    env.pub(s, f"{tag}ev{g}")

        run_stage(nc, sems,
                  [("sync", f_sync), ("vector", f_vector), ("tensor", f_tensor),
                   ("scalar", f_scalar)])
        pctx.close()

    def stage_proj_fm(tag, w_bf, b_sb, out_fm):
        pctx = ExitStack()
        pk = [pctx.enter_context(nc.psum_tensor(f"{tag}_pk{i}", [128, 512], F32))
              for i in range(2)]

        def f_tensor(env, t):
            for half in range(2):
                for sti in range(8):
                    g = half * 8 + sti
                    if g >= 2:
                        env.wait(t, f"{tag}act{g - 2}")
                    for ic in range(8):
                        t.matmul(pk[g % 2][:],
                                 w_bf[:, ic * 256 + half * 128:
                                      ic * 256 + half * 128 + 128],
                                 kv_fm[:, ic * KS + sti * 512:
                                       ic * KS + (sti + 1) * 512],
                                 start=(ic == 0), stop=(ic == 7))
                    env.mark(f"{tag}mm{g}", st_)

        def f_scalar(env, s):
            for half in range(2):
                for sti in range(8):
                    g = half * 8 + sti
                    env.wait(s, f"{tag}mm{g}")
                    s.activation(out_fm[:, half * KS + sti * 512:
                                        half * KS + (sti + 1) * 512],
                                 pk[g % 2][:], AF.Relu,
                                 bias=b_sb[:, half:half + 1])
                    env.pub(s, f"{tag}act{g}")

        run_stage(nc, sems, [("tensor", f_tensor), ("scalar", f_scalar)])
        pctx.close()

    stage_transpose_in("ke", keys, 32)
    stage_proj_fm("kp", wk_bf, bk_sb, k_fm)

    def s_kmax():
        pctx = ExitStack()
        pkn = pctx.enter_context(nc.psum_tensor("la_pkn", [1, 512], F32))

        def prev(hl, sti):
            return (hl, sti - 1) if sti >= 1 else (hl - 1, 7)

        def f_scalar(env, s):
            for hl in range(HL):
                r0 = 64 * (hl % 2)
                half = hl // 2
                for sti in range(8):
                    if (hl, sti) != (0, 0):
                        ph, pt_ = prev(hl, sti)
                        env.wait(s, f"km_t{ph}_{pt_}")
                    s.activation(sq_q[:], k_fm[r0:r0 + 64,
                                               half * KS + sti * 512:
                                               half * KS + (sti + 1) * 512],
                                 AF.Square)
                    env.pub(s, f"km_s{hl}_{sti}")

        def f_tensor(env, t):
            for hl in range(HL):
                for sti in range(8):
                    env.wait(t, f"km_s{hl}_{sti}")
                    if (hl, sti) != (0, 0):
                        ph, pt_ = prev(hl, sti)
                        env.wait(t, f"km_v{ph}_{pt_}")
                    t.matmul(pkn[0:1, :], ones64[:], sq_q[:],
                             start=True, stop=True)
                    env.mark(f"km_t{hl}_{sti}", st_)

        def f_vector(env, v):
            for hl in range(HL):
                for sti in range(8):
                    env.wait(v, f"km_t{hl}_{sti}")
                    v.reduce_max(out=qm2[:], in_=pkn[0:1, :], axis=AX.X)
                    v.tensor_max(out=kmax2[0:1, hl:hl + 1],
                                 in0=kmax2[0:1, hl:hl + 1], in1=qm2[:])
                    env.pub(v, f"km_v{hl}_{sti}")

        run_stage(nc, sems,
                  [("scalar", f_scalar), ("tensor", f_tensor), ("vector", f_vector)])
        pctx.close()

    s_kmax()

    stage_transpose_in("ve", values, 32)
    stage_proj_fm("vp", wv_bf, bv_sb, v_pfm)

    def s_vaug():
        pctx = ExitStack()
        pt = [pctx.enter_context(nc.psum_tensor(f"va_pt{i}", [128, 128], BF16))
              for i in range(2)]

        def f_tensor(env, t):
            for half in range(2):
                for sc_ in range(32):
                    g = half * 32 + sc_
                    if g >= 2:
                        env.wait(t, f"va_ev{g - 2}")
                    t.transpose(pt[g % 2][:],
                                v_pfm[:, half * KS + sc_ * 128:
                                      half * KS + (sc_ + 1) * 128],
                                ident[:])
                    env.mark(f"va_tr{g}", st_)

        def f_scalar(env, s):
            for half in range(2):
                for sc_ in range(32):
                    g = half * 32 + sc_
                    env.wait(s, f"va_tr{g}")
                    c0 = (sc_ * HL + half * 2) * 65
                    s.copy(v_aug[:, c0:c0 + 64], pt[g % 2][:, 0:64])
                    c1 = (sc_ * HL + half * 2 + 1) * 65
                    s.copy(v_aug[:, c1:c1 + 64], pt[g % 2][:, 64:128])
                    env.pub(s, f"va_ev{g}")

        run_stage(nc, sems, [("tensor", f_tensor), ("scalar", f_scalar)])
        pctx.close()

    s_vaug()

    ctx1.close()
    q_fm = sb("la_qfm", [128, 2 * S], BF16)

    def s_a2():
        pctx = ExitStack()
        pt = [pctx.enter_context(nc.psum_tensor(f"a2_pt{i}", [128, 128], BF16))
              for i in range(2)]
        pq = [pctx.enter_context(nc.psum_tensor(f"a2_pq{i}", [128, 512], F32))
              for i in range(2)]

        def f_sync(env, sync):
            for i in range(128):
                if i >= 2:
                    env.wait(sync, f"a2ln{i - 2}")
                    env.wait(sync, f"a2ssq{i - 2}")
                env.inc(sync.dma_start(out=xstg[i % 2][:],
                                       in_=xfull[i * 128:(i + 1) * 128, :]), ld)
                env.mark(f"a2ld{i}", ld)

        def f_vector(env, v):
            for i in range(128):
                env.wait(v, f"a2ld{i}")
                if i >= 2:
                    env.wait(v, f"a2tr{i - 2}_7")
                ln_vector(env, v, "a2", i, xstg[i % 2][:], ws, out=xbf[i % 2][:])

        def f_scalar(env, s):
            for g in range(32):
                for it in range(4):
                    i = g * 4 + it
                    env.wait(s, f"a2ld{i}")
                    ln_scalar(env, s, "a2", i, xstg[i % 2][:], ws)
                    for jc in range(8):
                        env.wait(s, f"a2tr{i}_{jc}")
                        s.copy(lnT[:, jc * 512 + it * 128:
                                   jc * 512 + (it + 1) * 128],
                               pt[(i * 8 + jc) % 2][:])
                        env.pub(s, f"a2ev{i}_{jc}")
                for half in range(2):
                    env.wait(s, f"a2qmm{g}_{half}")
                    s.activation(q_fm[:, half * S + g * 512:
                                      half * S + (g + 1) * 512],
                                 pq[half][:], AF.Relu,
                                 bias=bq_sb[:, half:half + 1])
                    env.pub(s, f"a2qact{g}_{half}")

        def f_tensor(env, t):
            for g in range(32):
                for it in range(4):
                    i = g * 4 + it
                    env.wait(t, f"a2ln{i}")
                    for jc in range(8):
                        gt = i * 8 + jc
                        if gt >= 2:
                            env.wait(t, f"a2ev{(gt - 2) // 8}_{(gt - 2) % 8}")
                        t.transpose(pt[gt % 2][:],
                                    xbf[i % 2][:, jc * 128:(jc + 1) * 128],
                                    ident[:])
                        env.mark(f"a2tr{i}_{jc}", st_)
                for half in range(2):
                    if g >= 1:
                        env.wait(t, f"a2qact{g - 1}_{half}")
                    env.wait(t, f"a2ev{g * 4 + 3}_7")
                    for ic in range(8):
                        t.matmul(pq[half][:],
                                 wq_bf[:, ic * 256 + half * 128:
                                       ic * 256 + half * 128 + 128],
                                 lnT[:, ic * 512:(ic + 1) * 512],
                                 start=(ic == 0), stop=(ic == 7))
                    env.mark(f"a2qmm{g}_{half}", st_)

        run_stage(nc, sems,
                  [("sync", f_sync), ("vector", f_vector), ("tensor", f_tensor),
                   ("scalar", f_scalar)])
        pctx.close()

    s_a2()

    def s_a2r():
        def f_sync(env, sync):
            for i in range(32):
                if i >= 2:
                    env.wait(sync, f"r_v{i - 2}")
                    env.wait(sync, f"r_ssq{i - 2}")
                env.inc(sync.dma_start(out=xstg[i % 2][:],
                                       in_=xown[i * 128:(i + 1) * 128, :]), ld)
                env.mark(f"r_ld{i}", ld)

        def f_vector(env, v):
            for i in range(32):
                env.wait(v, f"r_ld{i}")
                if i >= 1:
                    env.wait(v, f"r_st{i - 1}")
                ln_vector(env, v, "r_", i, xstg[i % 2][:], ws, out=xln32[:])
                v.tensor_add(out=r32[:], in0=xstg[i % 2][:], in1=xln32[:])
                env.pub(v, f"r_v{i}")

        def f_scalar(env, s):
            for i in range(32):
                env.wait(s, f"r_ld{i}")
                ln_scalar(env, s, "r_", i, xstg[i % 2][:], ws)

        def f_gp(env, g):
            for i in range(32):
                env.wait(g, f"r_v{i}")
                env.inc(g.dma_start(out=r_scr[i * 128:(i + 1) * 128, :],
                                    in_=r32[:]), sems["st"])
                env.inc(g.dma_start(out=r_scr[i * 128:(i + 1) * 128, :],
                                    in_=r32[:]), sems["st"])
                env.mark(f"r_st{i}", sems["st"])

        run_stage(nc, sems,
                  [("sync", f_sync), ("vector", f_vector), ("scalar", f_scalar),
                   ("gpsimd", f_gp)])

    s_a2r()

    def s_a3():
        pctx = ExitStack()
        psc = [pctx.enter_context(nc.psum_tensor(f"a3_ps{i}", [128, 1024], F32))
               for i in range(2)]
        pav = pctx.enter_context(nc.psum_tensor("a3_pav", [65, 512], F32))
        pqn = pctx.enter_context(nc.psum_tensor("a3_pqn", [1, 512], F32))
        pnc = pctx.enter_context(nc.psum_tensor("a3_pnc", [128, 1], F32))
        pz = pctx.enter_context(nc.psum_tensor("a3_pz", [64, 512], F32))

        def it_iter():
            for hl in range(HL):
                for qt in range(32):
                    yield hl, qt

        def _prev(hl, qt):
            return f"{hl}_{qt - 1}" if qt >= 1 else f"{hl - 1}_31"

        def _prevn(hl, qt, k):
            n = hl * 32 + qt - k
            return f"{n // 32}_{n % 32}"

        def f_scalar(env, s):
            for hl, qt in it_iter():
                key = f"{hl}_{qt}"
                r0 = 64 * (hl % 2)
                half = hl // 2
                if (hl, qt) != (0, 0):
                    env.wait(s, f"a3qnmm_{_prev(hl, qt)}")
                s.activation(sq_q[:],
                             q_fm[r0:r0 + 64, half * S + qt * 512:
                                  half * S + (qt + 1) * 512],
                             AF.Square)
                env.pub(s, f"a3sq_{key}")
                env.wait(s, f"a3b2_{key}")
                s.activation(lc[:], b2t[:], AF.Ln)
                s.activation(lc2[:], lc[:], AF.Exp, scale=0.5)
                s.mul(negc1[:], lc2[:], -0.125)
                env.pub(s, f"a3c_{key}")
                env.wait(s, f"a3ncmm_{key}")
                s.copy(negc_sb[:], pnc[:])
                env.pub(s, f"a3nc_{key}")
                for gk in range(16):
                    env.wait(s, f"a3s_{key}_{gk}")
                    if gk >= 2:
                        env.wait(s, f"a3pv_{key}_{gk - 2}")
                    s.activation(p_bf[gk % 2][:], psc[gk % 2][:], AF.Exp,
                                 bias=negc_sb[:], scale=0.125)
                    env.pub(s, f"a3e_{key}_{gk}")
                env.wait(s, f"a3pv_{key}_15")
                env.wait(s, f"a3zb_{key}")
                if (hl, qt) != (0, 0):
                    env.wait(s, f"a3n_{_prev(hl, qt)}")
                s.copy(av_sb[:], pav[0:64, :])
                s.copy(av_sb[:], pav[0:64, :])
                env.pub(s, f"a3av_{key}")

        def f_tensor(env, t):
            for hl, qt in it_iter():
                key = f"{hl}_{qt}"
                r0 = 64 * (hl % 2)
                half = hl // 2
                env.wait(t, f"a3sq_{key}")
                if (hl, qt) != (0, 0):
                    env.wait(t, f"a3b2_{_prev(hl, qt)}")
                t.matmul(pqn[0:1, :], ones64[:], sq_q[:], start=True, stop=True)
                env.mark(f"a3qnmm_{key}", st_)
                env.wait(t, f"a3c_{key}")
                if (hl, qt) != (0, 0):
                    env.wait(t, f"a3nc_{_prev(hl, qt)}")
                t.matmul(pnc[:], ones1f[:], negc1[:], start=True, stop=True)
                env.mark(f"a3ncmm_{key}", st_)
                for gk in range(16):
                    if gk >= 2:
                        env.wait(t, f"a3e_{key}_{gk - 2}")
                    elif (hl, qt) != (0, 0):
                        env.wait(t, f"a3e_{_prev(hl, qt)}_{14 + gk}")
                    for kk in range(2):
                        kc = gk * 2 + kk
                        t.matmul(psc[gk % 2][:, kk * 512:(kk + 1) * 512],
                                 k_fm[r0:r0 + 64, half * KS + kc * 128:
                                      half * KS + (kc + 1) * 128],
                                 q_fm[r0:r0 + 64, half * S + qt * 512:
                                      half * S + (qt + 1) * 512],
                                 start=True, stop=True)
                    env.mark(f"a3s_{key}_{gk}", st_)
                    env.wait(t, f"a3e_{key}_{gk}")
                    if gk == 0 and (hl, qt) != (0, 0):
                        env.wait(t, f"a3av_{_prev(hl, qt)}")
                        env.wait(t, f"a3z_{_prev(hl, qt)}")
                    for kk in range(2):
                        kc = gk * 2 + kk
                        c0 = (kc * HL + hl) * 65
                        t.matmul(pav[:], v_aug[:, c0:c0 + 65],
                                 p_bf[gk % 2][:, kk * 512:(kk + 1) * 512],
                                 start=(kc == 0), stop=(kc == 31))
                    env.mark(f"a3pv_{key}_{gk}", st_)
                env.wait(t, f"a3z_{key}")
                if (hl, qt) != (0, 0):
                    env.wait(t, f"a3n_{_prev(hl, qt)}")
                t.matmul(pz[:], ones1f[0:1, 0:64], zrow[:], start=True, stop=True)
                env.mark(f"a3zb_{key}", st_)

        def f_vector(env, v):
            for hl, qt in it_iter():
                key = f"{hl}_{qt}"
                env.wait(v, f"a3qnmm_{key}")
                v.reduce_max(out=qm2[:], in_=pqn[0:1, :], axis=AX.X)
                v.tensor_mul(out=b2t[:], in0=qm2[:], in1=kmax2[0:1, hl:hl + 1])
                env.pub(v, f"a3b2_{key}")
                env.wait(v, f"a3pv_{key}_15")
                v.reciprocal(out=zrow[:], in_=pav[64:65, :])
                env.pub(v, f"a3z_{key}")
                env.wait(v, f"a3zb_{key}")
                env.wait(v, f"a3av_{key}")
                idx = (hl * 32 + qt) % 4
                if hl * 32 + qt >= 4:
                    env.wait(v, f"a3wr_{_prevn(hl, qt, 4)}")
                v.tensor_mul(out=nrm_bf[idx][:], in0=av_sb[:], in1=pz[:])
                env.pub(v, f"a3n_{key}")

        def f_gp(env, g):
            for hl, qt in it_iter():
                key = f"{hl}_{qt}"
                env.wait(g, f"a3n_{key}")
                idx = (hl * 32 + qt) % 4
                env.inc(g.dma_start(out=a_scr[hl * 64:(hl + 1) * 64,
                                              qt * 512:(qt + 1) * 512],
                                    in_=nrm_bf[idx][:]), sems["st"])
                env.inc(g.dma_start(out=a_scr[hl * 64:(hl + 1) * 64,
                                              qt * 512:(qt + 1) * 512],
                                    in_=nrm_bf[idx][:]), sems["st"])
                env.mark(f"a3wr_{key}", sems["st"])

        run_stage(nc, sems,
                  [("scalar", f_scalar), ("tensor", f_tensor), ("vector", f_vector),
                   ("gpsimd", f_gp)])
        pctx.close()

    s_a3()

    a_tl = [sb(f"la_atl{i}", [128, D], F32) for i in range(2)]
    x1t = [sb(f"la_x1t{i}", [128, D], F32) for i in range(2)]

    def s_a4():
        def f_sync(env, sync):
            for tt in range(32):
                hl = tt // 8
                d0 = (tt % 8) * 8
                if tt >= 2:
                    env.wait(sync, f"a4v{tt - 2}")
                src = bass.AP(a_scr_t, (hl * 64 + d0) * S,
                              [[S, 8], [1024, 16], [1, 1024]])
                env.inc(sync.dma_start(out=a_tl[tt % 2][:], in_=src), ld)
                env.inc(sync.dma_start(out=xstg[tt % 2][:],
                                       in_=r_scr[tt * 128:(tt + 1) * 128, :]), ld)
                env.mark(f"a4ld{tt}", ld)
            env.inc(sync.dma_start(out=ws["sqscr"][0:1, 0:1024],
                                   in_=a_scr[0:1, 0:1024]), ld)
            env.mark("a4ld32", ld)

        def f_vector(env, v):
            for tt in range(32):
                env.wait(v, f"a4ld{tt}")
                env.wait(v, f"a4ld{min(tt + 1, 32)}")
                if tt >= 2:
                    env.wait(v, f"a4st{tt - 2}")
                v.tensor_add(out=x1t[tt % 2][:], in0=xstg[tt % 2][:],
                             in1=a_tl[tt % 2][:])
                env.pub(v, f"a4v{tt}")

        def f_gp(env, g):
            for tt in range(32):
                env.wait(g, f"a4v{tt}")
                env.inc(g.dma_start(out=x1out[tt * 128:(tt + 1) * 128, :],
                                    in_=x1t[tt % 2][:]), sems["st"])
                env.inc(g.dma_start(out=x1out[tt * 128:(tt + 1) * 128, :],
                                    in_=x1t[tt % 2][:]), sems["st"])
                env.mark(f"a4st{tt}", sems["st"])

        run_stage(nc, sems,
                  [("sync", f_sync), ("vector", f_vector), ("gpsimd", f_gp)])

    s_a4()

    if debug_outs is not None:
        def f_dbg(env, g):
            pairs = [
                (debug_outs["d_ascr"], a_scr[0:64, 0:512]),
                (debug_outs["d_kfm"], k_fm[:, 0:512]),
                (debug_outs["d_qfm"], q_fm[:, 0:512]),
                (debug_outs["d_vaug"], v_aug[:, 0:260]),
                (debug_outs["d_kmax2"], kmax2[:]),
                (debug_outs["d_negc"], negc_sb[:]),
                (debug_outs["d_lnT"], lnT[:, 0:512]),
            ]
            for dst, src in pairs:
                env.inc(g.dma_start(out=dst, in_=src), sems["st"])
        run_stage(nc, sems, [("gpsimd", f_dbg)])



def emit_l0_ffn(nc, ctx, x_in, wgT, wuT, wdT, x2out):
    sb = lambda n, shp, dt: ctx.enter_context(nc.sbuf_tensor(n, shp, dt))
    sems = mk_sems(nc, ctx, "lb", ["ld", "st", "v", "s", "t", "g"])
    ld, sst = sems["ld"], sems["st"]
    sv, ss, st_, sg = sems["v"], sems["s"], sems["t"], sems["g"]

    wd_scr_t = nc.dram_tensor("lb_wd_scr", [128, 8 * 24 * 128], BF16, kind="Internal")
    wd_scr = wd_scr_t.ap()

    ident = sb("lb_ident", [128, 128], BF16)
    wg_bf = sb("lb_wg", [128, 8 * FF0], BF16)
    wu_bf = sb("lb_wu", [128, 8 * FF0], BF16)
    wstg = [sb(f"lb_wstg{i}", [128, 1536], I32) for i in range(2)]
    wdt = [sb(f"lb_wdt{i}", [128, 24 * 128], BF16) for i in range(2)]
    xg = [sb(f"lb_xg{i}", [128, D], F32) for i in range(4)]
    xbf = [sb(f"lb_xbf{i}", [128, D], BF16) for i in range(2)]
    lnT = sb("lb_lnT", [128, 8 * 512], BF16)
    u_bf = sb("lb_u", [128, 512], BF16)
    gs32 = sb("lb_gs", [128, 512], F32)
    h_bf = sb("lb_h", [128, 24 * 512], BF16)
    y_fm = sb("lb_yfm", [128, 8 * 512], BF16)
    x2t = [sb(f"lb_x2t{i}", [128, D], F32) for i in range(2)]
    ws = {
        "sqscr": sb("lb_sqscr", [128, D], F32),
        "mu": sb("lb_mu", [128, 1], F32),
        "mu2": sb("lb_mu2", [128, 1], F32),
        "ssq": sb("lb_ssq", [128, 1], F32),
        "var": sb("lb_var", [128, 1], F32),
        "lnv": sb("lb_lnv", [128, 1], F32),
        "rstd": sb("lb_rstd", [128, 1], F32),
        "eps": sb("lb_eps", [128, 1], F32),
        "sv": sv, "ss": ss,
    }

    def s0_gpsimd(env, g):
        make_identity(g, ident[:])
        g.memset(ws["eps"][:], EPS_LN)

    def wtiles():
        k = 0
        for mi, (src, dst) in enumerate([(wgT, wg_bf), (wuT, wu_bf)]):
            for ic in range(8):
                for hf in range(2):
                    yield ("gu", k, src[ic * 128:(ic + 1) * 128,
                                        hf * 1536:(hf + 1) * 1536],
                           dst[:, ic * FF0 + hf * 1536:ic * FF0 + (hf + 1) * 1536],
                           1536, None)
                    k += 1
        for hc in range(24):
            yield ("d", k, wdT[hc * 128:(hc + 1) * 128, :], None, 1024, hc)
            k += 1

    def s0_sync(env, sync):
        for tag, k, src, dst, n, hc in wtiles():
            if k >= 2:
                env.wait(sync, f"lbwc{k - 2}")
            env.inc(sync.dma_start(out=wstg[k % 2][:, 0:n], in_=src), ld)
            env.mark(f"lbwl{k}", ld)

    def s0_vector(env, v):
        for tag, k, src, dst, n, hc in wtiles():
            env.wait(v, f"lbwl{k}")
            if tag == "gu":
                v.tensor_scalar_mul(out=dst, in0=wstg[k % 2][:, 0:n],
                                    scalar1=1.0 / 127.0)
                env.pub(v, f"lbwc{k}")
            else:
                if hc >= 2:
                    env.wait(v, f"lbwd{hc - 2}")
                v.tensor_scalar_mul(out=xbf[hc % 2][:], in0=wstg[k % 2][:, 0:n],
                                    scalar1=1.0 / 127.0)
                env.pub(v, f"lbwc{k}")

    def s0_gp2(env, g):
        for hc in range(24):
            env.wait(g, f"lbwc{32 + hc}")
            dst = bass.AP(wd_scr_t, hc * 128,
                          [[24 * 8 * 128, 128], [3072, 8], [1, 128]])
            env.inc(g.dma_start(out=dst, in_=xbf[hc % 2][:]), sst)
            env.inc(g.dma_start(out=dst, in_=xbf[hc % 2][:]), sst)
            env.mark(f"lbwd{hc}", sst)

    def s0_gp_all(env, g):
        s0_gpsimd(env, g)
        s0_gp2(env, g)

    run_stage(nc, sems,
              [("sync", s0_sync), ("vector", s0_vector), ("gpsimd", s0_gp_all)])

    def s1():
        pctx = ExitStack()
        pu = [pctx.enter_context(nc.psum_tensor(f"lb_pu{i}", [128, 512], F32))
              for i in range(2)]
        pg = [pctx.enter_context(nc.psum_tensor(f"lb_pg{i}", [128, 512], F32))
              for i in range(2)]
        pd = [pctx.enter_context(nc.psum_tensor(f"lb_pd{i}", [128, 512], F32))
              for i in range(2)]
        ptr = [pctx.enter_context(nc.psum_tensor(f"lb_ptr{i}", [128, 128], BF16))
               for i in range(2)]

        def f_sync(env, sync):
            for g in range(8):
                for it in range(4):
                    if g >= 1:
                        env.wait(sync, f"b_addend{g - 1}")
                    env.inc(sync.dma_start(
                        out=xg[it][:],
                        in_=x_in[(g * 4 + it) * 128:(g * 4 + it + 1) * 128, :]), ld)
                    env.mark(f"b_ld{g}_{it}", ld)
                for dc in range(8):
                    gi = g * 8 + dc
                    if gi >= 2:
                        pg_, pdc = divmod(gi - 2, 8)
                        env.wait(sync, f"b_dmm{pg_}_{pdc}")
                    env.inc(sync.dma_start(
                        out=wdt[gi % 2][:],
                        in_=wd_scr[:, dc * 3072:(dc + 1) * 3072]), ld)
                    env.mark(f"b_wdld{g}_{dc}", ld)

        def f_vector(env, v):
            for g in range(8):
                for it in range(4):
                    i = g * 4 + it
                    env.wait(v, f"b_ld{g}_{it}")
                    if i >= 2:
                        env.wait(v, f"b_tr{i - 2}_7")
                    ln_vector(env, v, "b_", i, xg[it][:], ws, out=xbf[i % 2][:])
                for hc in range(24):
                    if hc == 0 and g >= 1:
                        env.wait(v, f"b_dmm{g - 1}_7")
                    env.wait(v, f"b_sil{g}_{hc}")
                    v.tensor_mul(out=h_bf[:, hc * 512:(hc + 1) * 512],
                                 in0=gs32[:], in1=u_bf[:])
                    env.pub(v, f"b_h{g}_{hc}")
                for sub in range(4):
                    for dc in range(8):
                        env.wait(v, f"b_ytr{g}_{sub}_{dc}")
                        if dc == 0 and (g * 4 + sub) >= 2:
                            pi = g * 4 + sub - 2
                            env.wait(v, f"b_x2st{pi // 4}_{pi % 4}")
                        v.tensor_add(out=x2t[sub % 2][:, dc * 128:(dc + 1) * 128],
                                     in0=xg[sub][:, dc * 128:(dc + 1) * 128],
                                     in1=ptr[(sub * 8 + dc) % 2][:])
                        env.pub(v, f"b_add{g}_{sub}_{dc}")
                env.pub(v, f"b_addend{g}")

        def f_scalar(env, s):
            for g in range(8):
                for it in range(4):
                    i = g * 4 + it
                    env.wait(s, f"b_ld{g}_{it}")
                    ln_scalar(env, s, "b_", i, xg[it][:], ws)
                    for jc in range(8):
                        env.wait(s, f"b_tr{i}_{jc}")
                        s.copy(lnT[:, jc * 512 + it * 128:
                                   jc * 512 + (it + 1) * 128],
                               ptr[(i * 8 + jc) % 2][:])
                        env.pub(s, f"b_ev{i}_{jc}")
                for hc in range(24):
                    if hc >= 1:
                        env.wait(s, f"b_h{g}_{hc - 1}")
                    elif g >= 1:
                        env.wait(s, f"b_h{g - 1}_23")
                    env.wait(s, f"b_umm{g}_{hc}")
                    s.copy(u_bf[:], pu[hc % 2][:])
                    env.pub(s, f"b_u{g}_{hc}")
                    env.wait(s, f"b_gmm{g}_{hc}")
                    s.activation(gs32[:], pg[hc % 2][:], AF.Silu)
                    env.pub(s, f"b_sil{g}_{hc}")
                for dc in range(8):
                    env.wait(s, f"b_dmm{g}_{dc}")
                    s.copy(y_fm[:, dc * 512:(dc + 1) * 512], pd[dc % 2][:])
                    env.pub(s, f"b_y{g}_{dc}")

        def f_tensor(env, t):
            for g in range(8):
                for it in range(4):
                    i = g * 4 + it
                    env.wait(t, f"b_ln{i}")
                    for jc in range(8):
                        gt = i * 8 + jc
                        loc = it * 8 + jc
                        if loc >= 2:
                            pi = gt - 2
                            env.wait(t, f"b_ev{pi // 8}_{pi % 8}")
                        elif g >= 1:
                            env.wait(t, f"b_add{g - 1}_3_{6 + loc}")
                        t.transpose(ptr[gt % 2][:],
                                    xbf[i % 2][:, jc * 128:(jc + 1) * 128],
                                    ident[:])
                        env.mark(f"b_tr{i}_{jc}", st_)
                for hc in range(24):
                    if hc == 0:
                        env.wait(t, f"b_ev{g * 4 + 3}_7")
                    if hc >= 2:
                        env.wait(t, f"b_u{g}_{hc - 2}")
                        env.wait(t, f"b_sil{g}_{hc - 2}")
                    elif g >= 1:
                        env.wait(t, f"b_u{g - 1}_{22 + hc}")
                        env.wait(t, f"b_sil{g - 1}_{22 + hc}")
                    for ic in range(8):
                        t.matmul(pu[hc % 2][:],
                                 wu_bf[:, ic * FF0 + hc * 128:
                                       ic * FF0 + (hc + 1) * 128],
                                 lnT[:, ic * 512:(ic + 1) * 512],
                                 start=(ic == 0), stop=(ic == 7))
                    env.mark(f"b_umm{g}_{hc}", st_)
                    for ic in range(8):
                        t.matmul(pg[hc % 2][:],
                                 wg_bf[:, ic * FF0 + hc * 128:
                                       ic * FF0 + (hc + 1) * 128],
                                 lnT[:, ic * 512:(ic + 1) * 512],
                                 start=(ic == 0), stop=(ic == 7))
                    env.mark(f"b_gmm{g}_{hc}", st_)
                env.wait(t, f"b_h{g}_23")
                for dc in range(8):
                    gi = g * 8 + dc
                    env.wait(t, f"b_wdld{g}_{dc}")
                    if dc >= 2:
                        env.wait(t, f"b_y{g}_{dc - 2}")
                    elif g >= 1:
                        env.wait(t, f"b_y{g - 1}_{6 + dc}")
                    for hc in range(24):
                        t.matmul(pd[dc % 2][:],
                                 wdt[gi % 2][:, hc * 128:(hc + 1) * 128],
                                 h_bf[:, hc * 512:(hc + 1) * 512],
                                 start=(hc == 0), stop=(hc == 23))
                    env.mark(f"b_dmm{g}_{dc}", st_)
                for sub in range(4):
                    for dc in range(8):
                        gt = sub * 8 + dc
                        env.wait(t, f"b_y{g}_{dc}") if sub == 0 else None
                        if gt >= 2:
                            pi = gt - 2
                            env.wait(t, f"b_add{g}_{pi // 8}_{pi % 8}")
                        t.transpose(ptr[gt % 2][:],
                                    y_fm[:, dc * 512 + sub * 128:
                                         dc * 512 + (sub + 1) * 128],
                                    ident[:])
                        env.mark(f"b_ytr{g}_{sub}_{dc}", st_)

        def f_gp(env, g_):
            for g in range(8):
                for sub in range(4):
                    env.wait(g_, f"b_add{g}_{sub}_7")
                    env.inc(g_.dma_start(
                        out=x2out[(g * 4 + sub) * 128:(g * 4 + sub + 1) * 128, :],
                        in_=x2t[sub % 2][:]), sst)
                    env.inc(g_.dma_start(
                        out=x2out[(g * 4 + sub) * 128:(g * 4 + sub + 1) * 128, :],
                        in_=x2t[sub % 2][:]), sst)
                    env.mark(f"b_x2st{g}_{sub}", sst)

        run_stage(nc, sems,
                  [("sync", f_sync), ("vector", f_vector), ("scalar", f_scalar),
                   ("tensor", f_tensor), ("gpsimd", f_gp)])
        pctx.close()

    s1()



def emit_l1_attn(nc, ctx, x2full, x2own, wq, wk, wv, wo, w1, w2,
                 bq_c, bo_c, b1_c, b2_c, bk_row, bv_row,
                 g1_row, be1_row, g2_row, be2_row, x4out):
    sb = lambda n, shp, dt: ctx.enter_context(nc.sbuf_tensor(n, shp, dt))
    sems = mk_sems(nc, ctx, "lc", ["ld", "st", "v", "s", "t", "g"])
    ld, sst = sems["ld"], sems["st"]
    sv, ss, st_, sg = sems["v"], sems["s"], sems["t"], sems["g"]

    w1_scr_t = nc.dram_tensor("lc_w1_scr", [128, 32 * 8 * 128], BF16, kind="Internal")
    w2_scr_t = nc.dram_tensor("lc_w2_scr", [128, 8 * 32 * 128], BF16, kind="Internal")
    lnown_scr = nc.dram_tensor("lc_lnown_scr", [128, 8 * M], BF16, kind="Internal").ap()
    qfm_scr = nc.dram_tensor("lc_qfm_scr", [128, 8 * M], BF16, kind="Internal").ap()
    t2_scr = nc.dram_tensor("lc_t2_scr", [M, D], F32, kind="Internal").ap()

    ident = sb("lc_ident", [128, 128], BF16)
    ones1f = sb("lc_ones1f", [1, 128], F32)
    ones128 = sb("lc_ones128", [128, 1], BF16)
    bkb = sb("lc_bkb", [128, D], F32)
    bvb = sb("lc_bvb", [128, D], F32)
    g1b = sb("lc_g1b", [128, D], F32)
    be1b = sb("lc_be1b", [128, D], F32)
    g2b = sb("lc_g2b", [128, D], F32)
    be2b = sb("lc_be2b", [128, D], F32)
    bq_sb = sb("lc_bq", [128, 8], F32)
    bo_sb = sb("lc_bo", [128, 8], F32)
    b1_sb = sb("lc_b1", [128, 32], F32)
    b2_sb = sb("lc_b2", [128, 8], F32)
    kvaug = sb("lc_kvaug", [128, 8 * 65], BF16)
    ksrow = sb("lc_ksrow", [1, D], F32)
    ksrow_bf = sb("lc_ksrowbf", [1, D], BF16)

    xstg = [sb(f"lc_xstg{i}", [128, D], F32) for i in range(2)]
    xbf = [sb(f"lc_xbf{i}", [128, D], BF16) for i in range(2)]
    lnT = sb("lc_lnT", [128, 8 * 512], BF16)
    ws = {
        "sqscr": sb("lc_sqscr", [128, D], F32),
        "mu": sb("lc_mu", [128, 1], F32),
        "mu2": sb("lc_mu2", [128, 1], F32),
        "ssq": sb("lc_ssq", [128, 1], F32),
        "var": sb("lc_var", [128, 1], F32),
        "lnv": sb("lc_lnv", [128, 1], F32),
        "rstd": sb("lc_rstd", [128, 1], F32),
        "eps": sb("lc_eps", [128, 1], F32),
        "sv": sv, "ss": ss,
    }
    e_t = sb("lc_et", [128, 512], F32)
    e_m = sb("lc_em", [128, 512], F32)
    e_e = sb("lc_ee", [128, 512], F32)
    e_r = sb("lc_er", [128, 512], F32)
    xln4 = sb("lc_xln4", [128, D], F32)

    def c0():
        wctx = ExitStack()
        wstg = [wctx.enter_context(nc.sbuf_tensor(f"c0_wstg{i}", [128, FF1], F32))
                for i in range(2)]
        wcast = [wctx.enter_context(nc.sbuf_tensor(f"c0_wc{i}", [128, FF1], BF16))
                 for i in range(2)]
        pctx = ExitStack()
        pb = pctx.enter_context(nc.psum_tensor("lc_pb", [128, 512], F32))
        rows = [(bk_row, bkb), (bv_row, bvb), (g1_row, g1b), (be1_row, be1b),
                (g2_row, g2b), (be2_row, be2b)]

        def wtiles():
            for ic in range(8):
                yield (ic, w1[ic * 128:(ic + 1) * 128, :], FF1,
                       bass.AP(w1_scr_t, ic * 128,
                               [[32 * 8 * 128, 128], [8 * 128, 32], [1, 128]]))
            for fc in range(32):
                yield (8 + fc, w2[fc * 128:(fc + 1) * 128, :], D,
                       bass.AP(w2_scr_t, fc * 128,
                               [[8 * 32 * 128, 128], [32 * 128, 8], [1, 128]]))

        def f_sync(env, sync):
            for ri, (src, dst) in enumerate(rows):
                if ri >= 1:
                    env.wait(sync, f"c0bc{ri - 1}_1")
                env.inc(sync.dma_start(out=ws["sqscr"][0:1, :], in_=src), ld)
                env.mark(f"c0row{ri}", ld)
            for name, src, dst, n in [("bq", bq_c, bq_sb, 8), ("bo", bo_c, bo_sb, 8),
                                      ("b1", b1_c, b1_sb, 32), ("b2", b2_c, b2_sb, 8)]:
                for c in range(n):
                    env.inc(sync.dma_start(out=dst[:, c:c + 1],
                                           in_=src[c * 128:(c + 1) * 128, 0:1]), ld)
            for k, src, n, dst in wtiles():
                if k >= 2:
                    env.wait(sync, f"c0wc{k - 2}")
                env.inc(sync.dma_start(out=wstg[k % 2][:, 0:n], in_=src), ld)
                env.mark(f"c0wl{k}", ld)

        def f_tensor(env, t):
            for ri in range(len(rows)):
                env.wait(t, f"c0row{ri}")
                if ri >= 1:
                    env.wait(t, f"c0bc{ri - 1}_1")
                for half in range(2):
                    if ri == 0 and half == 1:
                        env.wait(t, f"c0bc{ri}_0")
                    elif half == 1:
                        env.wait(t, f"c0bc{ri}_0")
                    t.matmul(pb[:], ones1f[:],
                             ws["sqscr"][0:1, half * 512:(half + 1) * 512],
                             start=True, stop=True)
                    env.mark(f"c0bm{ri}_{half}", st_)

        def f_scalar(env, s):
            for ri, (src, dst) in enumerate(rows):
                for half in range(2):
                    env.wait(s, f"c0bm{ri}_{half}")
                    s.copy(dst[:, half * 512:(half + 1) * 512], pb[:])
                    env.pub(s, f"c0bc{ri}_{half}")

        def f_vector(env, v):
            for k, src, n, dst in wtiles():
                env.wait(v, f"c0wl{k}")
                if k >= 2:
                    env.wait(v, f"c0wst{k - 2}")
                v.tensor_copy(out=wcast[k % 2][:, 0:n], in_=wstg[k % 2][:, 0:n])
                env.pub(v, f"c0wc{k}")

        def f_gp(env, g):
            make_identity(g, ident[:])
            g.memset(ones1f[:], 1.0)
            g.memset(ones128[:], 1.0)
            g.memset(ws["eps"][:], EPS_LN)
            for k, src, n, dst in wtiles():
                env.wait(g, f"c0wc{k}")
                env.inc(g.dma_start(out=dst, in_=wcast[k % 2][:, 0:n]), sst)
                env.inc(g.dma_start(out=dst, in_=wcast[k % 2][:, 0:n]), sst)
                env.mark(f"c0wst{k}", sst)

        run_stage(nc, sems,
                  [("sync", f_sync), ("tensor", f_tensor), ("scalar", f_scalar),
                   ("vector", f_vector), ("gpsimd", f_gp)])
        pctx.close()
        wctx.close()

    c0()

    def c0b():
        pctx = ExitStack()
        ptr = [pctx.enter_context(nc.psum_tensor(f"c0b_pt{i}", [128, 128], BF16))
               for i in range(2)]

        def f_sync(env, sync):
            for i in range(32):
                if i >= 2:
                    env.wait(sync, f"o_ln{i - 2}")
                    env.wait(sync, f"o_ssq{i - 2}")
                env.inc(sync.dma_start(out=xstg[i % 2][:],
                                       in_=x2own[i * 128:(i + 1) * 128, :]), ld)
                env.mark(f"o_ld{i}", ld)

        def f_vector(env, v):
            for i in range(32):
                env.wait(v, f"o_ld{i}")
                if i >= 2:
                    env.wait(v, f"o_tr{i - 2}_7")
                ln_vector(env, v, "o_", i, xstg[i % 2][:], ws, out=xbf[i % 2][:])

        def f_scalar(env, s):
            for i in range(32):
                env.wait(s, f"o_ld{i}")
                ln_scalar(env, s, "o_", i, xstg[i % 2][:], ws)
                for jc in range(8):
                    env.wait(s, f"o_tr{i}_{jc}")
                    if i >= 1:
                        env.wait(s, f"o_st{i - 1}")
                    s.copy(lnT[:, jc * 512 + (i % 4) * 128:
                               jc * 512 + (i % 4 + 1) * 128],
                           ptr[(i * 8 + jc) % 2][:])
                    env.pub(s, f"o_ev{i}_{jc}")

        def f_tensor(env, t):
            for i in range(32):
                env.wait(t, f"o_ln{i}")
                for jc in range(8):
                    gt = i * 8 + jc
                    if gt >= 2:
                        pi = gt - 2
                        env.wait(t, f"o_ev{pi // 8}_{pi % 8}")
                    t.transpose(ptr[gt % 2][:],
                                xbf[i % 2][:, jc * 128:(jc + 1) * 128],
                                ident[:])
                    env.mark(f"o_tr{i}_{jc}", st_)

        def f_gp(env, g):
            for i in range(32):
                env.wait(g, f"o_ev{i}_7")
                for jc in range(8):
                    env.inc(g.dma_start(
                        out=lnown_scr[:, jc * M + i * 128:jc * M + (i + 1) * 128],
                        in_=lnT[:, jc * 512 + (i % 4) * 128:
                                jc * 512 + (i % 4 + 1) * 128]), sst)
                env.mark(f"o_st{i}", sst)

        run_stage(nc, sems,
                  [("sync", f_sync), ("vector", f_vector), ("scalar", f_scalar),
                   ("tensor", f_tensor), ("gpsimd", f_gp)])
        pctx.close()

    c0b()

    def c12():
        wctx = ExitStack()
        wk_bf = wctx.enter_context(nc.sbuf_tensor("lc_wk", [128, 8 * D], BF16))
        wv_bf = wctx.enter_context(nc.sbuf_tensor("lc_wv", [128, 8 * D], BF16))
        K_bf = [wctx.enter_context(nc.sbuf_tensor(f"lc_K{i}", [128, D], BF16))
                for i in range(2)]
        V_bf = [wctx.enter_context(nc.sbuf_tensor(f"lc_V{i}", [128, D], BF16))
                for i in range(2)]
        pctx = ExitStack()
        ptr = [pctx.enter_context(nc.psum_tensor(f"c1_pt{i}", [128, 128], BF16))
               for i in range(2)]
        pp = [pctx.enter_context(nc.psum_tensor(f"c1_pp{i}", [128, 512], F32))
              for i in range(2)]
        pkv = pctx.enter_context(nc.psum_tensor("c1_pkv", [64, 1024], F32))
        pks = pctx.enter_context(nc.psum_tensor("c1_pks", [1, 1024], F32))

        wstg2 = [ws["sqscr"], xln4]

        def f_sync_w(env, sync):
            for mi, w in enumerate([wk, wv]):
                for ic in range(8):
                    k = mi * 8 + ic
                    if k >= 2:
                        env.wait(sync, f"c1wc{k - 2}")
                    env.inc(sync.dma_start(out=wstg2[k % 2][:],
                                           in_=w[ic * 128:(ic + 1) * 128, :]), ld)
                    env.mark(f"c1wl{k}", ld)

        def f_vec_w(env, v):
            for mi, wdst in enumerate([wk_bf, wv_bf]):
                for ic in range(8):
                    k = mi * 8 + ic
                    env.wait(v, f"c1wl{k}")
                    v.tensor_copy(out=wdst[:, ic * D:(ic + 1) * D],
                                  in_=wstg2[k % 2][:])
                    env.pub(v, f"c1wc{k}")

        run_stage(nc, sems, [("sync", f_sync_w), ("vector", f_vec_w)])

        def f_sync(env, sync):
            for i in range(128):
                if i >= 2:
                    env.wait(sync, f"c_ln{i - 2}")
                    env.wait(sync, f"c_ssq{i - 2}")
                env.inc(sync.dma_start(out=xstg[i % 2][:],
                                       in_=x2full[i * 128:(i + 1) * 128, :]), ld)
                env.mark(f"c_ld{i}", ld)

        def f_vector(env, v):
            for i in range(128):
                env.wait(v, f"c_ld{i}")
                if i >= 2:
                    env.wait(v, f"c_tr{i - 2}_7")
                ln_vector(env, v, "c_", i, xstg[i % 2][:], ws, out=xbf[i % 2][:])
                bi = i % 2
                for half in range(2):
                    sl = slice(half * 512, (half + 1) * 512)
                    env.wait(v, f"c_kmm{i}_{half}")
                    if i >= 2:
                        env.wait(v, f"c_kvp{i - 2}")
                    v.tensor_add(out=e_t[:], in0=pp[half][:], in1=bkb[:, sl])
                    v.tensor_scalar_min(out=e_m[:], in0=e_t[:], scalar1=0.0)
                    env.pub(v, f"c_km{i}_{half}")
                    env.wait(v, f"c_ke{i}_{half}")
                    v.tensor_scalar_max(out=e_r[:], in0=e_t[:], scalar1=0.0)
                    v.tensor_add(out=K_bf[bi][:, sl], in0=e_e[:], in1=e_r[:])
                    env.pub(v, f"c_kd{i}_{half}")
                for half in range(2):
                    sl = slice(half * 512, (half + 1) * 512)
                    env.wait(v, f"c_vmm{i}_{half}")
                    v.tensor_add(out=V_bf[bi][:, sl], in0=pp[half][:], in1=bvb[:, sl])
                    env.pub(v, f"c_vd{i}_{half}")

        def f_scalar(env, s):
            for i in range(128):
                env.wait(s, f"c_ld{i}")
                ln_scalar(env, s, "c_", i, xstg[i % 2][:], ws)
                for jc in range(8):
                    env.wait(s, f"c_tr{i}_{jc}")
                    s.copy(lnT[:, jc * 512 + (i % 4) * 128:
                               jc * 512 + (i % 4 + 1) * 128],
                           ptr[(i * 8 + jc) % 2][:])
                    env.pub(s, f"c_ev{i}_{jc}")
                for half in range(2):
                    env.wait(s, f"c_km{i}_{half}")
                    s.activation(e_e[:], e_m[:], AF.Exp)
                    env.pub(s, f"c_ke{i}_{half}")

        def f_tensor(env, t):
            for i in range(128):
                env.wait(t, f"c_ln{i}")
                for jc in range(8):
                    gt = i * 8 + jc
                    if gt >= 2:
                        pi = gt - 2
                        env.wait(t, f"c_ev{pi // 8}_{pi % 8}")
                    t.transpose(ptr[gt % 2][:],
                                xbf[i % 2][:, jc * 128:(jc + 1) * 128],
                                ident[:])
                    env.mark(f"c_tr{i}_{jc}", st_)
                env.wait(t, f"c_ev{i}_7")
                it = i % 4
                for half in range(2):
                    if i >= 1:
                        env.wait(t, f"c_vd{i - 1}_{half}")
                    for ic in range(8):
                        t.matmul(pp[half][:],
                                 lnT[:, ic * 512 + it * 128:
                                     ic * 512 + (it + 1) * 128],
                                 wk_bf[:, ic * D + half * 512:
                                       ic * D + (half + 1) * 512],
                                 start=(ic == 0), stop=(ic == 7))
                    env.mark(f"c_kmm{i}_{half}", st_)
                for half in range(2):
                    env.wait(t, f"c_kd{i}_{half}")
                    for ic in range(8):
                        t.matmul(pp[half][:],
                                 lnT[:, ic * 512 + it * 128:
                                     ic * 512 + (it + 1) * 128],
                                 wv_bf[:, ic * D + half * 512:
                                       ic * D + (half + 1) * 512],
                                 start=(ic == 0), stop=(ic == 7))
                    env.mark(f"c_vmm{i}_{half}", st_)
                env.wait(t, f"c_kd{i}_1")
                env.wait(t, f"c_vd{i}_1")
                bi = i % 2
                for h in range(16):
                    t.matmul(pkv[0:64, h * 64:(h + 1) * 64],
                             K_bf[bi][:, h * 64:(h + 1) * 64],
                             V_bf[bi][:, h * 64:(h + 1) * 64],
                             start=(i == 0), stop=(i == 127))
                for half in range(2):
                    t.matmul(pks[0:1, half * 512:(half + 1) * 512],
                             ones128[:],
                             K_bf[bi][:, half * 512:(half + 1) * 512],
                             start=(i == 0), stop=(i == 127))
                env.mark(f"c_kvp{i}", st_)

        run_stage(nc, sems,
                  [("sync", f_sync), ("vector", f_vector), ("scalar", f_scalar),
                   ("tensor", f_tensor)])

        def f2_scalar(env, s):
            for h in range(16):
                r0 = 64 * (h % 2)
                cb = (h // 2) * 65
                s.copy(kvaug[r0:r0 + 64, cb:cb + 64],
                       pkv[0:64, h * 64:(h + 1) * 64])
            s.copy(ksrow[0:1, 0:512], pks[0:1, 0:512])
            s.copy(ksrow[0:1, 512:1024], pks[0:1, 512:1024])
            env.pub(s, "c2ks")
            for cc in range(8):
                env.wait(s, f"c2kt{cc}")
                s.copy(kvaug[0:64, cc * 65 + 64:cc * 65 + 65],
                       ptr[cc % 2][0:64, 0:1])
                s.copy(kvaug[64:128, cc * 65 + 64:cc * 65 + 65],
                       ptr[cc % 2][64:128, 0:1])
                env.pub(s, f"c2kc{cc}")

        def f2_vector(env, v):
            env.wait(v, "c2ks")
            v.tensor_copy(out=ksrow_bf[:], in_=ksrow[:])
            env.pub(v, "c2ksbf")

        def f2_tensor(env, t):
            env.wait(t, "c2ksbf")
            for cc in range(8):
                if cc >= 2:
                    env.wait(t, f"c2kc{cc - 2}")
                t.transpose(ptr[cc % 2][:, 0:1],
                            ksrow_bf[0:1, cc * 128:(cc + 1) * 128], ident[0:1, 0:1])
                env.mark(f"c2kt{cc}", st_)

        run_stage(nc, sems,
                  [("scalar", f2_scalar), ("vector", f2_vector),
                   ("tensor", f2_tensor)])
        pctx.close()
        wctx.close()

    c12()

    def c3():
        wctx = ExitStack()
        wq_bf = wctx.enter_context(nc.sbuf_tensor("lc_wq", [128, 8 * D], BF16))
        lnt3 = [wctx.enter_context(nc.sbuf_tensor(f"c3_ln{i}", [128, 8 * 512], BF16))
                for i in range(2)]
        qtile = [wctx.enter_context(nc.sbuf_tensor(f"c3_q{i}", [128, 512], BF16))
                 for i in range(2)]
        pctx = ExitStack()
        pq = [pctx.enter_context(nc.psum_tensor(f"c3_pq{i}", [128, 512], F32))
              for i in range(2)]

        wstg2 = [ws["sqscr"], xln4]

        def f_sync_w(env, sync):
            for ic in range(8):
                if ic >= 2:
                    env.wait(sync, f"c3wc{ic - 2}")
                env.inc(sync.dma_start(out=wstg2[ic % 2][:],
                                       in_=wq[ic * 128:(ic + 1) * 128, :]), ld)
                env.mark(f"c3wl{ic}", ld)

        def f_vec_w(env, v):
            for ic in range(8):
                env.wait(v, f"c3wl{ic}")
                v.tensor_copy(out=wq_bf[:, ic * D:(ic + 1) * D],
                              in_=wstg2[ic % 2][:])
                env.pub(v, f"c3wc{ic}")

        run_stage(nc, sems, [("sync", f_sync_w), ("vector", f_vec_w)])

        def f_sync(env, sync):
            for tg in range(8):
                if tg >= 2:
                    env.wait(sync, f"c3qmm{tg - 2}_7")
                for ic in range(8):
                    env.inc(sync.dma_start(
                        out=lnt3[tg % 2][:, ic * 512:(ic + 1) * 512],
                        in_=lnown_scr[:, ic * M + tg * 512:ic * M + (tg + 1) * 512]),
                        ld)
                env.mark(f"c3lnl{tg}", ld)

        def f_tensor(env, t):
            for tg in range(8):
                env.wait(t, f"c3lnl{tg}")
                for oc in range(8):
                    g = tg * 8 + oc
                    if g >= 2:
                        pg_, poc = divmod(g - 2, 8)
                        env.wait(t, f"c3m{pg_}_{poc}")
                    for ic in range(8):
                        t.matmul(pq[g % 2][:],
                                 wq_bf[:, ic * D + oc * 128:ic * D + (oc + 1) * 128],
                                 lnt3[tg % 2][:, ic * 512:(ic + 1) * 512],
                                 start=(ic == 0), stop=(ic == 7))
                    env.mark(f"c3qmm{tg}_{oc}", st_)

        def f_vector(env, v):
            for tg in range(8):
                for oc in range(8):
                    g = tg * 8 + oc
                    env.wait(v, f"c3qmm{tg}_{oc}")
                    v.tensor_scalar_add(out=e_t[:], in0=pq[g % 2][:],
                                        scalar1=bq_sb[:, oc:oc + 1])
                    v.tensor_scalar_min(out=e_m[:], in0=e_t[:], scalar1=0.0)
                    env.pub(v, f"c3m{tg}_{oc}")
                    env.wait(v, f"c3e{tg}_{oc}")
                    if g >= 2:
                        env.wait(v, f"c3qst{(g - 2) // 8}_{(g - 2) % 8}")
                    v.tensor_scalar_max(out=e_r[:], in0=e_t[:], scalar1=0.0)
                    v.tensor_add(out=qtile[g % 2][:], in0=e_e[:], in1=e_r[:])
                    env.pub(v, f"c3qd{tg}_{oc}")

        def f_scalar(env, s):
            for tg in range(8):
                for oc in range(8):
                    env.wait(s, f"c3m{tg}_{oc}")
                    s.activation(e_e[:], e_m[:], AF.Exp)
                    env.pub(s, f"c3e{tg}_{oc}")

        def f_gp(env, g_):
            for tg in range(8):
                for oc in range(8):
                    g = tg * 8 + oc
                    env.wait(g_, f"c3qd{tg}_{oc}")
                    env.inc(g_.dma_start(
                        out=qfm_scr[:, oc * M + tg * 512:oc * M + (tg + 1) * 512],
                        in_=qtile[g % 2][:]), sst)
                    env.mark(f"c3qst{tg}_{oc}", sst)

        run_stage(nc, sems,
                  [("sync", f_sync), ("tensor", f_tensor), ("vector", f_vector),
                   ("scalar", f_scalar), ("gpsimd", f_gp)])
        pctx.close()
        wctx.close()

    c3()

    def c4a():
        wctx = ExitStack()
        wo_bf = wctx.enter_context(nc.sbuf_tensor("lc_wo", [128, 8 * D], BF16))
        qtg = [wctx.enter_context(nc.sbuf_tensor(f"c4_qtg{i}", [128, 8 * 512], BF16))
               for i in range(2)]
        av_fm = wctx.enter_context(nc.sbuf_tensor("lc_avfm", [128, 8 * 512], BF16))
        g_sb = wctx.enter_context(nc.sbuf_tensor("lc_gsb", [64, 512], F32))
        zr = wctx.enter_context(nc.sbuf_tensor("lc_zr", [1, 512], F32))
        zr2 = wctx.enter_context(nc.sbuf_tensor("lc_zr2", [1, 512], F32))
        o_bf = wctx.enter_context(nc.sbuf_tensor("lc_obf", [128, 8 * 512], BF16))
        t1t = [wctx.enter_context(nc.sbuf_tensor(f"lc_t1t{i}", [128, D], F32))
               for i in range(4)]
        t2t = [wctx.enter_context(nc.sbuf_tensor(f"lc_t2t{i}", [128, D], F32))
               for i in range(2)]
        pctx = ExitStack()
        pG = [pctx.enter_context(nc.psum_tensor(f"c4_pG{i}", [65, 512], F32))
              for i in range(2)]
        pzb = pctx.enter_context(nc.psum_tensor("c4_pzb", [64, 512], F32))
        po = [pctx.enter_context(nc.psum_tensor(f"c4_po{i}", [128, 512], F32))
              for i in range(2)]
        ptr = [pctx.enter_context(nc.psum_tensor(f"c4_pt{i}", [128, 128], BF16))
               for i in range(2)]

        wstg2 = [ws["sqscr"], xln4]

        def f_sync_w(env, sync):
            for ic in range(8):
                if ic >= 2:
                    env.wait(sync, f"c4wc{ic - 2}")
                env.inc(sync.dma_start(out=wstg2[ic % 2][:],
                                       in_=wo[ic * 128:(ic + 1) * 128, :]), ld)
                env.mark(f"c4wl{ic}", ld)

        def f_vec_w(env, v):
            for ic in range(8):
                env.wait(v, f"c4wl{ic}")
                v.tensor_copy(out=wo_bf[:, ic * D:(ic + 1) * D],
                              in_=wstg2[ic % 2][:])
                env.pub(v, f"c4wc{ic}")

        run_stage(nc, sems, [("sync", f_sync_w), ("vector", f_vec_w)])

        def f_sync(env, sync):
            for tg in range(8):
                if tg >= 2:
                    env.wait(sync, f"c4G{tg - 2}_15")
                for oc in range(8):
                    env.inc(sync.dma_start(
                        out=qtg[tg % 2][:, oc * 512:(oc + 1) * 512],
                        in_=qfm_scr[:, oc * M + tg * 512:oc * M + (tg + 1) * 512]),
                        ld)
                env.mark(f"c4ql{tg}", ld)
                for it in range(4):
                    i = tg * 4 + it
                    if i >= 2:
                        env.wait(sync, f"x4_ln{i - 2}")
                        env.wait(sync, f"x4_ssq{i - 2}")
                    env.inc(sync.dma_start(out=xstg[i % 2][:],
                                           in_=x2own[i * 128:(i + 1) * 128, :]), ld)
                    env.mark(f"c4xl{i}", ld)

        def f_tensor(env, t):
            for tg in range(8):
                env.wait(t, f"c4ql{tg}")
                for h in range(16):
                    if h >= 2:
                        env.wait(t, f"c4z{tg}_{h - 2}")
                        env.wait(t, f"c4gc{tg}_{h - 2}")
                    elif tg >= 1:
                        env.wait(t, f"c4z{tg - 1}_{14 + h}")
                        env.wait(t, f"c4gc{tg - 1}_{14 + h}")
                    r0 = 64 * (h % 2)
                    cb = (h // 2) * 65
                    t.matmul(pG[h % 2][:],
                             kvaug[r0:r0 + 64, cb:cb + 65],
                             qtg[tg % 2][r0:r0 + 64, (h // 2) * 512:
                                         (h // 2 + 1) * 512],
                             start=True, stop=True)
                    env.mark(f"c4G{tg}_{h}", st_)
                    env.wait(t, f"c4z2{tg}_{h}")
                    if (tg, h) != (0, 0):
                        ph, hh = (tg, h - 1) if h >= 1 else (tg - 1, 15)
                        env.wait(t, f"c4av{ph}_{hh}")
                    t.matmul(pzb[:], ones1f[0:1, 0:64], zr2[:],
                             start=True, stop=True)
                    env.mark(f"c4zb{tg}_{h}", st_)
                for oc2 in range(8):
                    if oc2 == 0:
                        env.wait(t, f"c4av{tg}_15")
                    if oc2 >= 2:
                        env.wait(t, f"c4oc{tg}_{oc2 - 2}")
                    elif tg >= 1:
                        env.wait(t, f"c4oc{tg - 1}_{6 + oc2}")
                    for ic in range(8):
                        t.matmul(po[oc2 % 2][:],
                                 wo_bf[:, ic * D + oc2 * 128:
                                       ic * D + (oc2 + 1) * 128],
                                 av_fm[:, ic * 512:(ic + 1) * 512],
                                 start=(ic == 0), stop=(ic == 7))
                    env.mark(f"c4omm{tg}_{oc2}", st_)
                for oc2 in range(8):
                    env.wait(t, f"c4oc{tg}_{oc2}")
                    for sub in range(4):
                        gt = oc2 * 4 + sub
                        if gt >= 2:
                            pi = gt - 2
                            env.wait(t, f"c4ta{tg}_{pi // 4}_{pi % 4}")
                        elif tg >= 1:
                            pi = 30 + gt
                            env.wait(t, f"c4ta{tg - 1}_{pi // 4}_{pi % 4}")
                        t.transpose(ptr[gt % 2][:],
                                    o_bf[:, oc2 * 512 + sub * 128:
                                         oc2 * 512 + (sub + 1) * 128],
                                    ident[:])
                        env.mark(f"c4otr{tg}_{oc2}_{sub}", st_)

        def f_vector(env, v):
            for tg in range(8):
                for it in range(4):
                    i = tg * 4 + it
                    env.wait(v, f"c4xl{i}")
                    ln_vector(env, v, "x4_", i, xstg[i % 2][:], ws, out=t1t[it][:])
                for h in range(16):
                    env.wait(v, f"c4G{tg}_{h}")
                    v.tensor_scalar_add(out=zr[:], in0=pG[h % 2][64:65, :],
                                        scalar1=EPS_ATTN)
                    env.pub(v, f"c4z{tg}_{h}")
                    v.reciprocal(out=zr2[:], in_=zr[:])
                    env.pub(v, f"c4z2{tg}_{h}")
                    env.wait(v, f"c4zb{tg}_{h}")
                    env.wait(v, f"c4gc{tg}_{h}")
                    r0 = 64 * (h % 2)
                    v.tensor_mul(out=av_fm[r0:r0 + 64, (h // 2) * 512:
                                           (h // 2 + 1) * 512],
                                 in0=g_sb[:], in1=pzb[:])
                    env.pub(v, f"c4av{tg}_{h}")
                for oc2 in range(8):
                    for sub in range(4):
                        env.wait(v, f"c4otr{tg}_{oc2}_{sub}")
                        sl = slice(oc2 * 128, (oc2 + 1) * 128)
                        v.tensor_add(out=t1t[sub][:, sl], in0=t1t[sub][:, sl],
                                     in1=ptr[(oc2 * 4 + sub) % 2][:])
                        env.pub(v, f"c4ta{tg}_{oc2}_{sub}")
                for sub in range(4):
                    i2 = tg * 4 + sub
                    ln_vector(env, v, "t1_", i2, t1t[sub][:], ws)
                    if i2 >= 2:
                        env.wait(v, f"c4t2st{(i2 - 2) // 4}_{(i2 - 2) % 4}")
                    v.tensor_scalar(out=xln4[:], in0=t1t[sub][:],
                                    scalar1=ws["mu"][:], scalar2=ws["rstd"][:],
                                    op0=ALU.subtract, op1=ALU.mult)
                    v.tensor_mul(out=xln4[:], in0=xln4[:], in1=g1b[:])
                    v.tensor_add(out=t2t[sub % 2][:], in0=xln4[:], in1=be1b[:])
                    env.pub(v, f"c4t2{tg}_{sub}")

        def f_scalar(env, s):
            for tg in range(8):
                for it in range(4):
                    i = tg * 4 + it
                    env.wait(s, f"c4xl{i}")
                    ln_scalar(env, s, "x4_", i, xstg[i % 2][:], ws)
                for h in range(16):
                    env.wait(s, f"c4G{tg}_{h}")
                    if (tg, h) != (0, 0):
                        ph, hh = (tg, h - 1) if h >= 1 else (tg - 1, 15)
                        env.wait(s, f"c4av{ph}_{hh}")
                    s.copy(g_sb[:], pG[h % 2][0:64, :])
                    env.pub(s, f"c4gc{tg}_{h}")
                for oc2 in range(8):
                    env.wait(s, f"c4omm{tg}_{oc2}")
                    if tg >= 1:
                        env.wait(s, f"c4otr{tg - 1}_{oc2}_3")
                    s.activation(o_bf[:, oc2 * 512:(oc2 + 1) * 512],
                                 po[oc2 % 2][:], AF.Identity,
                                 bias=bo_sb[:, oc2:oc2 + 1])
                    env.pub(s, f"c4oc{tg}_{oc2}")
                for sub in range(4):
                    i2 = tg * 4 + sub
                    env.wait(s, f"c4ta{tg}_7_{sub}")
                    ln_scalar(env, s, "t1_", i2, t1t[sub][:], ws)

        def f_gp(env, g_):
            for tg in range(8):
                for sub in range(4):
                    env.wait(g_, f"c4t2{tg}_{sub}")
                    i2 = tg * 4 + sub
                    env.inc(g_.dma_start(out=t2_scr[i2 * 128:(i2 + 1) * 128, :],
                                         in_=t2t[sub % 2][:]), sst)
                    env.inc(g_.dma_start(out=t2_scr[i2 * 128:(i2 + 1) * 128, :],
                                         in_=t2t[sub % 2][:]), sst)
                    env.mark(f"c4t2st{tg}_{sub}", sst)

        run_stage(nc, sems,
                  [("sync", f_sync), ("tensor", f_tensor), ("vector", f_vector),
                   ("scalar", f_scalar), ("gpsimd", f_gp)])
        pctx.close()
        wctx.close()

    c4a()

    def c4b():
        wctx = ExitStack()
        w1t = [wctx.enter_context(nc.sbuf_tensor(f"c4b_w1{i}", [128, 8 * 128], BF16))
               for i in range(2)]
        w2t = [wctx.enter_context(nc.sbuf_tensor(f"c4b_w2{i}", [128, 32 * 128], BF16))
               for i in range(2)]
        h1 = wctx.enter_context(nc.sbuf_tensor("c4b_h1", [128, 32 * 512], BF16))
        t2g = [wctx.enter_context(nc.sbuf_tensor(f"c4b_t2g{i}", [128, D], F32))
               for i in range(4)]
        y_bf = wctx.enter_context(nc.sbuf_tensor("c4b_y", [128, 8 * 512], BF16))
        xo = [wctx.enter_context(nc.sbuf_tensor(f"c4b_xo{i}", [128, D], F32))
              for i in range(2)]
        x4t = [wctx.enter_context(nc.sbuf_tensor(f"c4b_x4{i}", [128, D], F32))
               for i in range(2)]
        pctx = ExitStack()
        pf = [pctx.enter_context(nc.psum_tensor(f"c4b_pf{i}", [128, 512], F32))
              for i in range(2)]
        po2 = [pctx.enter_context(nc.psum_tensor(f"c4b_po{i}", [128, 512], F32))
               for i in range(2)]
        ptr = [pctx.enter_context(nc.psum_tensor(f"c4b_pt{i}", [128, 128], BF16))
               for i in range(2)]

        def f_sync(env, sync):
            for tg in range(8):
                for it in range(4):
                    i = tg * 4 + it
                    if tg >= 1:
                        env.wait(sync, f"f_x4{tg - 1}_{it}")
                    env.inc(sync.dma_start(out=t2g[it][:],
                                           in_=t2_scr[i * 128:(i + 1) * 128, :]), ld)
                    env.mark(f"f_t2l{tg}_{it}", ld)
                for fc in range(32):
                    gi = tg * 32 + fc
                    if gi >= 2:
                        pg_, pfc = divmod(gi - 2, 32)
                        env.wait(sync, f"f_fmm{pg_}_{pfc}")
                    env.inc(sync.dma_start(
                        out=w1t[gi % 2][:],
                        in_=bass.AP(w1_scr_t, fc * 8 * 128,
                                    [[32 * 8 * 128, 128], [1, 8 * 128]])), ld)
                    env.mark(f"f_w1l{tg}_{fc}", ld)
                for oc in range(8):
                    gi = tg * 8 + oc
                    if gi >= 2:
                        pg_, poc = divmod(gi - 2, 8)
                        env.wait(sync, f"f_omm{pg_}_{poc}")
                    env.inc(sync.dma_start(
                        out=w2t[gi % 2][:],
                        in_=bass.AP(w2_scr_t, oc * 32 * 128,
                                    [[8 * 32 * 128, 128], [1, 32 * 128]])), ld)
                    env.mark(f"f_w2l{tg}_{oc}", ld)
                for it in range(4):
                    i = tg * 4 + it
                    if i >= 2:
                        env.wait(sync, f"f_x4{(i - 2) // 4}_{(i - 2) % 4}")
                    env.inc(sync.dma_start(out=xo[i % 2][:],
                                           in_=x2own[i * 128:(i + 1) * 128, :]), ld)
                    env.mark(f"f_xol{i}", ld)

        def f_vector(env, v):
            for tg in range(8):
                for it in range(4):
                    i = tg * 4 + it
                    env.wait(v, f"f_t2l{tg}_{it}")
                    if i >= 2:
                        env.wait(v, f"f_tr{i - 2}_7")
                    v.tensor_copy(out=xbf[i % 2][:], in_=t2g[it][:])
                    env.pub(v, f"f_cast{i}")
                for sub in range(4):
                    for oc in range(8):
                        env.wait(v, f"f_ytr{tg}_{sub}_{oc}")
                        sl = slice(oc * 128, (oc + 1) * 128)
                        v.tensor_add(out=t2g[sub][:, sl], in0=t2g[sub][:, sl],
                                     in1=ptr[(sub * 8 + oc) % 2][:])
                        env.pub(v, f"f_ta{tg}_{sub}_{oc}")
                for sub in range(4):
                    i2 = tg * 4 + sub
                    ln_vector(env, v, "t3_", i2, t2g[sub][:], ws)
                    env.wait(v, f"f_xol{i2}")
                    if i2 >= 2:
                        env.wait(v, f"f_x4st{(i2 - 2) // 4}_{(i2 - 2) % 4}")
                    v.tensor_scalar(out=xln4[:], in0=t2g[sub][:],
                                    scalar1=ws["mu"][:], scalar2=ws["rstd"][:],
                                    op0=ALU.subtract, op1=ALU.mult)
                    v.tensor_mul(out=xln4[:], in0=xln4[:], in1=g2b[:])
                    v.tensor_add(out=xln4[:], in0=xln4[:], in1=be2b[:])
                    v.tensor_add(out=x4t[i2 % 2][:], in0=xo[i2 % 2][:], in1=xln4[:])
                    env.pub(v, f"f_x4{tg}_{sub}")

        def f_scalar(env, s):
            for tg in range(8):
                for it in range(4):
                    i = tg * 4 + it
                    for jc in range(8):
                        env.wait(s, f"f_tr{i}_{jc}")
                        s.copy(lnT[:, jc * 512 + it * 128:
                                   jc * 512 + (it + 1) * 128],
                               ptr[(i * 8 + jc) % 2][:])
                        env.pub(s, f"f_ev{i}_{jc}")
                for fc in range(32):
                    env.wait(s, f"f_fmm{tg}_{fc}")
                    if tg >= 1 and fc == 0:
                        env.wait(s, f"f_omm{tg - 1}_7")
                    s.activation(h1[:, fc * 512:(fc + 1) * 512], pf[fc % 2][:],
                                 AF.Relu, bias=b1_sb[:, fc:fc + 1])
                    env.pub(s, f"f_h{tg}_{fc}")
                for oc in range(8):
                    env.wait(s, f"f_omm{tg}_{oc}")
                    if tg >= 1:
                        env.wait(s, f"f_ytr{tg - 1}_3_{oc}")
                    s.activation(y_bf[:, oc * 512:(oc + 1) * 512], po2[oc % 2][:],
                                 AF.Identity, bias=b2_sb[:, oc:oc + 1])
                    env.pub(s, f"f_y{tg}_{oc}")
                for sub in range(4):
                    i2 = tg * 4 + sub
                    env.wait(s, f"f_ta{tg}_{sub}_7")
                    ln_scalar(env, s, "t3_", i2, t2g[sub][:], ws)

        def f_tensor(env, t):
            for tg in range(8):
                for it in range(4):
                    i = tg * 4 + it
                    env.wait(t, f"f_cast{i}")
                    for jc in range(8):
                        gt = i * 8 + jc
                        loc = it * 8 + jc
                        if loc >= 2:
                            pi = gt - 2
                            env.wait(t, f"f_ev{pi // 8}_{pi % 8}")
                        elif tg >= 1:
                            env.wait(t, f"f_ta{tg - 1}_3_{6 + loc}")
                        t.transpose(ptr[gt % 2][:],
                                    xbf[i % 2][:, jc * 128:(jc + 1) * 128],
                                    ident[:])
                        env.mark(f"f_tr{i}_{jc}", st_)
                for fc in range(32):
                    if fc == 0:
                        env.wait(t, f"f_ev{tg * 4 + 3}_7")
                    env.wait(t, f"f_w1l{tg}_{fc}")
                    if fc >= 2:
                        env.wait(t, f"f_h{tg}_{fc - 2}")
                    elif tg >= 1:
                        env.wait(t, f"f_h{tg - 1}_{30 + fc}")
                    for ic in range(8):
                        t.matmul(pf[fc % 2][:],
                                 w1t[(tg * 32 + fc) % 2][:, ic * 128:(ic + 1) * 128],
                                 lnT[:, ic * 512:(ic + 1) * 512],
                                 start=(ic == 0), stop=(ic == 7))
                    env.mark(f"f_fmm{tg}_{fc}", st_)
                for oc in range(8):
                    if oc == 0:
                        env.wait(t, f"f_h{tg}_31")
                    env.wait(t, f"f_w2l{tg}_{oc}")
                    if oc >= 2:
                        env.wait(t, f"f_y{tg}_{oc - 2}")
                    elif tg >= 1:
                        env.wait(t, f"f_y{tg - 1}_{6 + oc}")
                    for fc in range(32):
                        t.matmul(po2[oc % 2][:],
                                 w2t[(tg * 8 + oc) % 2][:, fc * 128:(fc + 1) * 128],
                                 h1[:, fc * 512:(fc + 1) * 512],
                                 start=(fc == 0), stop=(fc == 31))
                    env.mark(f"f_omm{tg}_{oc}", st_)
                for sub in range(4):
                    for oc in range(8):
                        gt = sub * 8 + oc
                        if sub == 0:
                            env.wait(t, f"f_y{tg}_{oc}")
                        if gt >= 2:
                            pi = gt - 2
                            env.wait(t, f"f_ta{tg}_{pi // 8}_{pi % 8}")
                        t.transpose(ptr[gt % 2][:],
                                    y_bf[:, oc * 512 + sub * 128:
                                         oc * 512 + (sub + 1) * 128],
                                    ident[:])
                        env.mark(f"f_ytr{tg}_{sub}_{oc}", st_)

        def f_gp(env, g_):
            for tg in range(8):
                for sub in range(4):
                    env.wait(g_, f"f_x4{tg}_{sub}")
                    i2 = tg * 4 + sub
                    env.inc(g_.dma_start(out=x4out[i2 * 128:(i2 + 1) * 128, :],
                                         in_=x4t[i2 % 2][:]), sst)
                    env.inc(g_.dma_start(out=x4out[i2 * 128:(i2 + 1) * 128, :],
                                         in_=x4t[i2 % 2][:]), sst)
                    env.mark(f"f_x4st{tg}_{sub}", sst)

        run_stage(nc, sems,
                  [("sync", f_sync), ("vector", f_vector), ("scalar", f_scalar),
                   ("tensor", f_tensor), ("gpsimd", f_gp)])
        pctx.close()
        wctx.close()

    c4b()



def emit_l1_moe(nc, ctx, x_in, wr, br_row, w1e, b1_c, w2e, b2_c, out):
    sb = lambda n, shp, dt: ctx.enter_context(nc.sbuf_tensor(n, shp, dt))
    sems = mk_sems(nc, ctx, "ld_", ["ld", "st", "v", "s", "t", "g", "wb"])
    ld, sst, swb = sems["ld"], sems["st"], sems["wb"]
    sv, ss, st_, sg = sems["v"], sems["s"], sems["t"], sems["g"]

    ln4_scr = nc.dram_tensor("ld_ln4_scr", [128, 8 * M], BF16, kind="Internal").ap()
    pickT_scr = nc.dram_tensor("ld_pick_scr", [8, M], F32, kind="Internal").ap()
    w_scr_t = [nc.dram_tensor(f"ld_wscr{i}", [128, 8 * FMOE + 32 * D], BF16,
                              kind="Internal") for i in range(2)]
    acc_scr = nc.dram_tensor("ld_acc_scr", [128, 8 * M], BF16, kind="Internal").ap()
    ym_trash = nc.dram_tensor("ld_ym_trash", [128, 512], BF16, kind="Internal").ap()

    ident = sb("ld_ident", [128, 128], BF16)
    ident32 = sb("ld_ident32", [128, 128], F32)
    ones1f = sb("ld_ones1f", [1, 128], F32)
    wr_bf = sb("ld_wr", [128, 8 * 8], BF16)
    brb = sb("ld_brb", [128, 8], F32)
    b1e = sb("ld_b1e", [128, 32], F32)
    b2e = sb("ld_b2e", [128, 8], F32)

    xstg = [sb(f"ld_xstg{i}", [128, D], F32) for i in range(2)]
    xbf = [sb(f"ld_xbf{i}", [128, D], BF16) for i in range(2)]
    lnT = sb("ld_lnT", [128, 8 * 512], BF16)
    ws = {
        "sqscr": sb("ld_sqscr", [128, D], F32),
        "mu": sb("ld_mu", [128, 1], F32),
        "mu2": sb("ld_mu2", [128, 1], F32),
        "ssq": sb("ld_ssq", [128, 1], F32),
        "var": sb("ld_var", [128, 1], F32),
        "lnv": sb("ld_lnv", [128, 1], F32),
        "rstd": sb("ld_rstd", [128, 1], F32),
        "eps": sb("ld_eps", [128, 1], F32),
        "sv": sv, "ss": ss,
    }
    s_tm = sb("ld_stm", [128, 8], F32)
    mrow = sb("ld_mrow", [128, 1], F32)
    nfound = sb("ld_nf", [128, 1], F32)
    eqc = sb("ld_eq", [128, 1], F32)
    pick_tm = sb("ld_pick", [128, 8], F32)
    pickT_sb = sb("ld_pickT", [8, 128], F32)
    prow = [sb(f"ld_prow{i}", [1, 512], F32) for i in range(2)]
    mask_bf = [sb(f"ld_mask{i}", [128, 512], BF16) for i in range(2)]

    wstg = [sb(f"ld_wstg{i}", [128, FMOE], F32) for i in range(2)]
    wcast = [sb(f"ld_wcast{i}", [128, FMOE], BF16) for i in range(2)]
    lnt_d = [sb(f"ld_lntd{i}", [128, 8 * 512], BF16) for i in range(2)]
    w1t = [sb(f"ld_w1t{i}", [128, 8 * 128], BF16) for i in range(2)]
    w2t = [sb(f"ld_w2t{i}", [128, 32 * 128], BF16) for i in range(2)]
    h1 = sb("ld_h1", [128, 32 * 512], BF16)
    ym = [sb(f"ld_ym{i}", [128, 512], BF16) for i in range(2)]
    abf = [sb(f"ld_abf{i}", [128, 8 * 128], BF16) for i in range(2)]
    outt = [sb(f"ld_outt{i}", [128, D], F32) for i in range(2)]

    def d0():
        pctx = ExitStack()
        pb = pctx.enter_context(nc.psum_tensor("d0_pb", [128, 8], F32))

        def f_gp(env, g):
            make_identity(g, ident[:])
            make_identity(g, ident32[:])
            g.memset(ws["eps"][:], EPS_LN)
            g.memset(ones1f[:], 1.0)

        def f_sync(env, sync):
            for ic in range(8):
                if ic >= 2:
                    env.wait(sync, f"d0wc{ic - 2}")
                env.inc(sync.dma_start(out=ws["sqscr"][:, 0:8],
                                       in_=wr[ic * 128:(ic + 1) * 128, :]), ld)
                env.mark(f"d0wl{ic}", ld)
            env.inc(sync.dma_start(out=ws["sqscr"][0:1, 8:16], in_=br_row), ld)
            env.mark("d0br", ld)

        def f_vector(env, v):
            for ic in range(8):
                env.wait(v, f"d0wl{ic}")
                v.tensor_copy(out=wr_bf[:, ic * 8:(ic + 1) * 8],
                              in_=ws["sqscr"][:, 0:8])
                env.pub(v, f"d0wc{ic}")

        def f_tensor(env, t):
            env.wait(t, "d0br")
            t.matmul(pb[:], ones1f[:], ws["sqscr"][0:1, 8:16],
                     start=True, stop=True)
            env.mark("d0bm", st_)

        def f_scalar(env, s):
            env.wait(s, "d0bm")
            s.copy(brb[:], pb[:])
            env.pub(s, "d0bc")

        run_stage(nc, sems,
                  [("gpsimd", f_gp), ("sync", f_sync), ("vector", f_vector),
                   ("tensor", f_tensor), ("scalar", f_scalar)])
        pctx.close()

    d0()

    def d1():
        pctx = ExitStack()
        ptr = [pctx.enter_context(nc.psum_tensor(f"d1_pt{i}", [128, 128], BF16))
               for i in range(2)]
        psc = pctx.enter_context(nc.psum_tensor("d1_psc", [128, 8], F32))
        ppk = pctx.enter_context(nc.psum_tensor("d1_ppk", [8, 128], F32))

        def f_sync(env, sync):
            for i in range(32):
                if i >= 2:
                    env.wait(sync, f"d_ln{i - 2}")
                    env.wait(sync, f"d_ssq{i - 2}")
                env.inc(sync.dma_start(out=xstg[i % 2][:],
                                       in_=x_in[i * 128:(i + 1) * 128, :]), ld)
                env.mark(f"d_ld{i}", ld)

        def f_vector(env, v):
            for i in range(32):
                env.wait(v, f"d_ld{i}")
                if i >= 2:
                    env.wait(v, f"d_tr{i - 2}_7")
                ln_vector(env, v, "d_", i, xstg[i % 2][:], ws, out=xbf[i % 2][:])
                env.wait(v, f"d_scmm{i}")
                v.tensor_add(out=s_tm[:], in0=psc[:], in1=brb[:])
                v.reduce_max(out=mrow[:], in_=s_tm[:], axis=AX.X)
                v.tensor_scalar(out=nfound[:], in0=mrow[:], scalar1=0.0,
                                scalar2=1.0, op0=ALU.mult, op1=ALU.add)
                for e in range(E):
                    v.tensor_scalar(out=eqc[:], in0=s_tm[:, e:e + 1],
                                    scalar1=mrow[:], scalar2=None, op0=ALU.is_ge)
                    v.tensor_mul(out=pick_tm[:, e:e + 1], in0=eqc[:], in1=nfound[:])
                    v.tensor_sub(out=nfound[:], in0=nfound[:],
                                 in1=pick_tm[:, e:e + 1])
                env.pub(v, f"d_pick{i}")

        def f_scalar(env, s):
            for i in range(32):
                env.wait(s, f"d_ld{i}")
                ln_scalar(env, s, "d_", i, xstg[i % 2][:], ws)
                for jc in range(8):
                    env.wait(s, f"d_tr{i}_{jc}")
                    if i >= 1:
                        env.wait(s, f"d_lnst{i - 1}")
                    s.copy(lnT[:, jc * 512 + (i % 4) * 128:
                               jc * 512 + (i % 4 + 1) * 128],
                           ptr[(i * 8 + jc) % 2][:])
                    env.pub(s, f"d_ev{i}_{jc}")
                env.wait(s, f"d_pktr{i}")
                if i >= 1:
                    env.wait(s, f"d_pkst{i - 1}")
                s.copy(pickT_sb[:], ppk[:])
                env.pub(s, f"d_pkc{i}")

        def f_tensor(env, t):
            for i in range(32):
                env.wait(t, f"d_ln{i}")
                for jc in range(8):
                    gt = i * 8 + jc
                    if gt >= 2:
                        pi = gt - 2
                        env.wait(t, f"d_ev{pi // 8}_{pi % 8}")
                    t.transpose(ptr[gt % 2][:],
                                xbf[i % 2][:, jc * 128:(jc + 1) * 128],
                                ident[:])
                    env.mark(f"d_tr{i}_{jc}", st_)
                env.wait(t, f"d_ev{i}_7")
                if i >= 1:
                    env.wait(t, f"d_pick{i - 1}")
                for ic in range(8):
                    t.matmul(psc[:],
                             lnT[:, ic * 512 + (i % 4) * 128:
                                 ic * 512 + (i % 4 + 1) * 128],
                             wr_bf[:, ic * 8:(ic + 1) * 8],
                             start=(ic == 0), stop=(ic == 7))
                env.mark(f"d_scmm{i}", st_)
                env.wait(t, f"d_pick{i}")
                if i >= 1:
                    env.wait(t, f"d_pkc{i - 1}")
                t.transpose(ppk[:], pick_tm[:], ident32[:])
                env.mark(f"d_pktr{i}", st_)

        def f_gp(env, g):
            for i in range(32):
                env.wait(g, f"d_ev{i}_7")
                for rpt in range(2):
                    for jc in range(8):
                        env.inc(g.dma_start(
                            out=ln4_scr[:, jc * M + i * 128:jc * M + (i + 1) * 128],
                            in_=lnT[:, jc * 512 + (i % 4) * 128:
                                    jc * 512 + (i % 4 + 1) * 128]), sst)
                env.mark(f"d_lnst{i}", sst)
                env.wait(g, f"d_pkc{i}")
                env.inc(g.dma_start(out=pickT_scr[0:8, i * 128:(i + 1) * 128],
                                    in_=pickT_sb[:]), sst)
                env.inc(g.dma_start(out=pickT_scr[0:8, i * 128:(i + 1) * 128],
                                    in_=pickT_sb[:]), sst)
                env.mark(f"d_pkst{i}", sst)

        run_stage(nc, sems,
                  [("sync", f_sync), ("vector", f_vector), ("scalar", f_scalar),
                   ("tensor", f_tensor), ("gpsimd", f_gp)])
        pctx.close()

    d1()

    def d2():
        pctx = ExitStack()
        pf = [pctx.enter_context(nc.psum_tensor(f"d2_pf{i}", [128, 512], F32))
              for i in range(2)]
        po2 = [pctx.enter_context(nc.psum_tensor(f"d2_po{i}", [128, 512], F32))
               for i in range(2)]
        pm = pctx.enter_context(nc.psum_tensor("d2_pm", [128, 512], F32))

        def wb_tiles(e):
            for ic in range(8):
                yield (ic, w1e[e][ic * 128:(ic + 1) * 128, :], FMOE,
                       bass.AP(w_scr_t[e % 2], ic * 128,
                               [[8 * FMOE + 32 * D, 128], [8 * 128, 32], [1, 128]]))
            for fc in range(32):
                yield (8 + fc, w2e[e][fc * 128:(fc + 1) * 128, :], D,
                       bass.AP(w_scr_t[e % 2], 8 * FMOE + fc * 128,
                               [[8 * FMOE + 32 * D, 128], [32 * 128, 8], [1, 128]]))

        def f_sync(env, sync):
            for e in range(E + 1):
                if e < E:
                    for k, src, n, dst in wb_tiles(e):
                        kk = e * 40 + k
                        if kk >= 2:
                            env.wait(sync, f"d2wc{kk - 2}")
                        env.inc(sync.dma_start(out=wstg[kk % 2][:, 0:n], in_=src),
                                ld)
                        env.mark(f"d2wl{kk}", ld)
                if e >= 1:
                    ce = e - 1
                    if ce >= 1:
                        env.wait(sync, f"d2h{ce * 8 - 1}_31")
                        env.wait(sync, f"d2y{ce * 8 - 1}_7")
                    for c in range(32):
                        env.inc(sync.dma_start(
                            out=b1e[:, c:c + 1],
                            in_=b1_c[ce * FMOE + c * 128:ce * FMOE + (c + 1) * 128,
                                     0:1]), ld)
                    for c in range(8):
                        env.inc(sync.dma_start(
                            out=b2e[:, c:c + 1],
                            in_=b2_c[ce * D + c * 128:ce * D + (c + 1) * 128,
                                     0:1]), ld)
                    env.mark(f"d2b{ce}", ld)
                    for tg in range(8):
                        gi = ce * 8 + tg
                        if tg == 0:
                            env.wait(sync, f"d2ws{ce * 40 + 39}")
                        if gi >= 2:
                            env.wait(sync, f"d2fmm{gi - 2}_31")
                        for jc in range(8):
                            env.inc(sync.dma_start(
                                out=lnt_d[gi % 2][:, jc * 512:(jc + 1) * 512],
                                in_=ln4_scr[:, jc * M + tg * 512:
                                            jc * M + (tg + 1) * 512]), ld)
                        env.inc(sync.dma_start(
                            out=prow[gi % 2][:],
                            in_=pickT_scr[ce:ce + 1, tg * 512:(tg + 1) * 512]), ld)
                        env.mark(f"d2ln{gi}", ld)
                        for fc in range(32):
                            fgi = gi * 32 + fc
                            if fgi >= 2:
                                env.wait(sync, f"d2fmm{(fgi - 2) // 32}_{(fgi - 2) % 32}")
                            env.inc(sync.dma_start(
                                out=w1t[fgi % 2][:],
                                in_=bass.AP(w_scr_t[ce % 2], fc * 8 * 128,
                                            [[8 * FMOE + 32 * D, 128],
                                             [1, 8 * 128]])), ld)
                            env.mark(f"d2w1l{fgi}", ld)
                        for oc in range(8):
                            ogi = gi * 8 + oc
                            if ogi >= 2:
                                env.wait(sync, f"d2omm{(ogi - 2) // 8}_{(ogi - 2) % 8}")
                            env.inc(sync.dma_start(
                                out=w2t[ogi % 2][:],
                                in_=bass.AP(w_scr_t[ce % 2],
                                            8 * FMOE + oc * 32 * 128,
                                            [[8 * FMOE + 32 * D, 128],
                                             [1, 32 * 128]])), ld)
                            env.mark(f"d2w2l{ogi}", ld)

        def f_vector(env, v):
            for e in range(E + 1):
                if e < E:
                    for k, src, n, dst in wb_tiles(e):
                        kk = e * 40 + k
                        env.wait(v, f"d2wl{kk}")
                        if kk >= 2:
                            env.wait(v, f"d2ws{kk - 2}")
                        v.tensor_copy(out=wcast[kk % 2][:, 0:n],
                                      in_=wstg[kk % 2][:, 0:n])
                        env.pub(v, f"d2wc{kk}")
                if e >= 1:
                    ce = e - 1
                    for tg in range(8):
                        gi = ce * 8 + tg
                        for oc in range(8):
                            ogi = gi * 8 + oc
                            env.wait(v, f"d2omm{gi}_{oc}")
                            env.wait(v, f"d2mk{gi}")
                            if ogi >= 2:
                                env.wait(v, f"d2yst{(ogi - 2) // 8}_{(ogi - 2) % 8}")
                            v.scalar_tensor_tensor(out=ym[ogi % 2][:],
                                                   in0=po2[oc % 2][:],
                                                   scalar=b2e[:, oc:oc + 1],
                                                   in1=mask_bf[gi % 2][:],
                                                   op0=ALU.add, op1=ALU.mult)
                            env.pub(v, f"d2y{gi}_{oc}")

        def f_scalar(env, s):
            for e in range(E):
                for tg in range(8):
                    gi = e * 8 + tg
                    env.wait(s, f"d2mbm{gi}")
                    if gi >= 2:
                        env.wait(s, f"d2y{gi - 2}_7")
                    s.copy(mask_bf[gi % 2][:], pm[:])
                    env.pub(s, f"d2mk{gi}")
                    for fc in range(32):
                        env.wait(s, f"d2fmm{gi}_{fc}")
                        if gi >= 1 and fc == 0:
                            env.wait(s, f"d2omm{gi - 1}_7")
                        s.activation(h1[:, fc * 512:(fc + 1) * 512], pf[fc % 2][:],
                                     AF.Gelu, bias=b1e[:, fc:fc + 1])
                        env.pub(s, f"d2h{gi}_{fc}")

        def f_tensor(env, t):
            for e in range(E):
                for tg in range(8):
                    gi = e * 8 + tg
                    env.wait(t, f"d2ln{gi}")
                    if gi >= 1:
                        env.wait(t, f"d2mk{gi - 1}")
                    t.matmul(pm[:], ones1f[:], prow[gi % 2][:], start=True, stop=True)
                    env.mark(f"d2mbm{gi}", st_)
                    for fc in range(32):
                        fgi = gi * 32 + fc
                        env.wait(t, f"d2w1l{fgi}")
                        if fc >= 2:
                            env.wait(t, f"d2h{gi}_{fc - 2}")
                        elif gi >= 1:
                            env.wait(t, f"d2h{gi - 1}_{30 + fc}")
                        if fc == 0:
                            env.wait(t, f"d2b{e}")
                        for ic in range(8):
                            t.matmul(pf[fc % 2][:],
                                     w1t[fgi % 2][:, ic * 128:(ic + 1) * 128],
                                     lnt_d[gi % 2][:, ic * 512:(ic + 1) * 512],
                                     start=(ic == 0), stop=(ic == 7))
                        env.mark(f"d2fmm{gi}_{fc}", st_)
                    for oc in range(8):
                        ogi = gi * 8 + oc
                        env.wait(t, f"d2w2l{ogi}")
                        if oc == 0:
                            env.wait(t, f"d2h{gi}_31")
                        if oc >= 2:
                            env.wait(t, f"d2y{gi}_{oc - 2}")
                        elif gi >= 1:
                            env.wait(t, f"d2y{gi - 1}_{6 + oc}")
                        for fc in range(32):
                            t.matmul(po2[oc % 2][:],
                                     w2t[ogi % 2][:, fc * 128:(fc + 1) * 128],
                                     h1[:, fc * 512:(fc + 1) * 512],
                                     start=(fc == 0), stop=(fc == 31))
                        env.mark(f"d2omm{gi}_{oc}", st_)

        def f_gp(env, g_):
            for e in range(E + 1):
                if e < E:
                    for k, src, n, dst in wb_tiles(e):
                        kk = e * 40 + k
                        if k == 0 and e >= 2:
                            env.wait(g_, f"d2w2l{(e - 2) * 64 + 63}")
                        env.wait(g_, f"d2wc{kk}")
                        env.inc(g_.dma_start(out=dst, in_=wcast[kk % 2][:, 0:n]),
                                swb)
                        env.inc(g_.dma_start(out=dst, in_=wcast[kk % 2][:, 0:n]),
                                swb)
                        env.mark(f"d2ws{kk}", swb)
                    env.mark(f"d2wdone{e}", swb)
                if e >= 1:
                    ce = e - 1
                    for tg in range(8):
                        gi = ce * 8 + tg
                        for oc in range(8):
                            ogi = gi * 8 + oc
                            env.wait(g_, f"d2y{gi}_{oc}")
                            if gi >= 8:
                                env.wait(g_, f"d2yst{gi - 8}_{oc}")
                            dst = acc_scr[:, oc * M + tg * 512:oc * M + (tg + 1) * 512]
                            if ce == 0:
                                env.inc(g_.dma_start(out=dst, in_=ym[ogi % 2][:]),
                                        sst)
                                env.inc(g_.dma_start(out=dst, in_=ym[ogi % 2][:]),
                                        sst)
                            else:
                                env.inc(g_.dma_start(out=ym_trash,
                                                     in_=ym[ogi % 2][:]), sst)
                                env.inc(g_.dma_start(out=dst, in_=ym[ogi % 2][:],
                                                     accum_op=ALU.add), sst)
                            env.mark(f"d2yst{gi}_{oc}", sst)
                        env.mark(f"d2dn{ce}", sst) if tg == 7 else None

        run_stage(nc, sems,
                  [("sync", f_sync), ("vector", f_vector), ("scalar", f_scalar),
                   ("tensor", f_tensor), ("gpsimd", f_gp)])
        pctx.close()

    d2()

    def d3():
        pctx = ExitStack()
        ptr = [pctx.enter_context(nc.psum_tensor(f"d3_pt{i}", [128, 128], BF16))
               for i in range(2)]

        def f_sync(env, sync):
            for i in range(32):
                if i >= 2:
                    env.wait(sync, f"d3o{i - 2}")
                env.inc(sync.dma_start(out=xstg[i % 2][:],
                                       in_=x_in[i * 128:(i + 1) * 128, :]), ld)
                if i >= 2:
                    env.wait(sync, f"d3tr{i - 2}_7")
                for oc in range(8):
                    env.inc(sync.dma_start(
                        out=abf[i % 2][:, oc * 128:(oc + 1) * 128],
                        in_=acc_scr[:, oc * M + i * 128:oc * M + (i + 1) * 128]),
                        ld)
                env.mark(f"d3l{i}", ld)

        def f_tensor(env, t):
            for i in range(32):
                env.wait(t, f"d3l{i}")
                for oc in range(8):
                    gt = i * 8 + oc
                    if gt >= 2:
                        pi = gt - 2
                        env.wait(t, f"d3a{pi // 8}_{pi % 8}")
                    t.transpose(ptr[gt % 2][:],
                                abf[i % 2][:, oc * 128:(oc + 1) * 128], ident[:])
                    env.mark(f"d3tr{i}_{oc}", st_)

        def f_vector(env, v):
            for i in range(32):
                for oc in range(8):
                    env.wait(v, f"d3tr{i}_{oc}")
                    if oc == 0 and i >= 2:
                        env.wait(v, f"d3st{i - 2}")
                    sl = slice(oc * 128, (oc + 1) * 128)
                    v.tensor_add(out=outt[i % 2][:, sl], in0=xstg[i % 2][:, sl],
                                 in1=ptr[(i * 8 + oc) % 2][:])
                    env.pub(v, f"d3a{i}_{oc}")
                env.pub(v, f"d3o{i}")

        def f_gp(env, g_):
            for i in range(32):
                env.wait(g_, f"d3o{i}")
                env.inc(g_.dma_start(out=out[i * 128:(i + 1) * 128, :],
                                     in_=outt[i % 2][:]), sst)
                env.inc(g_.dma_start(out=out[i * 128:(i + 1) * 128, :],
                                     in_=outt[i % 2][:]), sst)
                env.mark(f"d3st{i}", sst)

        run_stage(nc, sems,
                  [("sync", f_sync), ("tensor", f_tensor), ("vector", f_vector),
                   ("gpsimd", f_gp)])
        pctx.close()

    d3()



def build_nc_l0attn(debug=False):
    nc = bass.Bass("TRN2", target_bir_lowering=False)
    xfull = nc.dram_tensor("xfull", [S, D], F32, kind="ExternalInput").ap()
    xown = nc.dram_tensor("xown", [M, D], F32, kind="ExternalInput").ap()
    keys = nc.dram_tensor("keys", [KS, D], F32, kind="ExternalInput").ap()
    values = nc.dram_tensor("values", [KS, D], F32, kind="ExternalInput").ap()
    wqT = nc.dram_tensor("wqT", [D, 256], F32, kind="ExternalInput").ap()
    wkT = nc.dram_tensor("wkT", [D, 256], F32, kind="ExternalInput").ap()
    wvT = nc.dram_tensor("wvT", [D, 256], F32, kind="ExternalInput").ap()
    bq = nc.dram_tensor("bq", [256, 1], F32, kind="ExternalInput").ap()
    bk = nc.dram_tensor("bk", [256, 1], F32, kind="ExternalInput").ap()
    bv = nc.dram_tensor("bv", [256, 1], F32, kind="ExternalInput").ap()
    x1out = nc.dram_tensor("x1out", [M, D], F32, kind="ExternalOutput").ap()
    dbg = None
    if debug:
        dbg = {
            "d_r": nc.dram_tensor("d_r", [128, D], F32, kind="ExternalOutput").ap(),
            "d_ascr": nc.dram_tensor("d_ascr", [64, 512], BF16, kind="ExternalOutput").ap(),
            "d_kfm": nc.dram_tensor("d_kfm", [128, 512], BF16, kind="ExternalOutput").ap(),
            "d_qfm": nc.dram_tensor("d_qfm", [128, 512], BF16, kind="ExternalOutput").ap(),
            "d_vaug": nc.dram_tensor("d_vaug", [128, 260], BF16, kind="ExternalOutput").ap(),
            "d_kmax2": nc.dram_tensor("d_kmax2", [1, 4], F32, kind="ExternalOutput").ap(),
            "d_negc": nc.dram_tensor("d_negc", [128, 1], F32, kind="ExternalOutput").ap(),
            "d_lnT": nc.dram_tensor("d_lnT", [128, 512], BF16, kind="ExternalOutput").ap(),
        }
    ctx = ExitStack()
    emit_l0_attn(nc, ctx, xfull, xown, keys, values, wqT, wkT, wvT, bq, bk, bv,
                 x1out, debug_outs=dbg)
    ctx.close()
    return nc


def build_nc_l0ffn():
    nc = bass.Bass("TRN2", target_bir_lowering=False)
    x_in = nc.dram_tensor("xb_in", [M, D], F32, kind="ExternalInput").ap()
    wgT = nc.dram_tensor("wgT", [D, FF0], I32, kind="ExternalInput").ap()
    wuT = nc.dram_tensor("wuT", [D, FF0], I32, kind="ExternalInput").ap()
    wdT = nc.dram_tensor("wdT", [FF0, D], I32, kind="ExternalInput").ap()
    x2out = nc.dram_tensor("x2out", [M, D], F32, kind="ExternalOutput").ap()
    ctx = ExitStack()
    emit_l0_ffn(nc, ctx, x_in, wgT, wuT, wdT, x2out)
    ctx.close()
    return nc


def l0ffn_in_maps(inp, x1):
    wgT = np.ascontiguousarray(np.asarray(inp["L0_ffn_w_gate"]).T)
    wuT = np.ascontiguousarray(np.asarray(inp["L0_ffn_w_up"]).T)
    wdT = np.ascontiguousarray(np.asarray(inp["L0_ffn_w_down"]).T)
    in_maps = []
    for c in range(8):
        b, j = divmod(c, 4)
        in_maps.append({
            "xb_in": np.ascontiguousarray(x1[b, j * M:(j + 1) * M]),
            "wgT": wgT, "wuT": wuT, "wdT": wdT,
        })
    return in_maps


def run_l0ffn(inp, x1):
    nc = build_nc_l0ffn()
    res = run_bass_kernel_spmd(nc, l0ffn_in_maps(inp, x1), core_ids=list(range(8)))
    x2 = np.stack([res.results[c]["x2out"] for c in range(8)])
    return x2.reshape(B, S, D)


def build_nc_l1attn():
    nc = bass.Bass("TRN2", target_bir_lowering=False)
    x2full = nc.dram_tensor("x2full", [S, D], F32, kind="ExternalInput").ap()
    x2own = nc.dram_tensor("x2own", [M, D], F32, kind="ExternalInput").ap()
    wq = nc.dram_tensor("wq", [D, D], F32, kind="ExternalInput").ap()
    wk = nc.dram_tensor("wk", [D, D], F32, kind="ExternalInput").ap()
    wv = nc.dram_tensor("wv", [D, D], F32, kind="ExternalInput").ap()
    wo = nc.dram_tensor("wo", [D, D], F32, kind="ExternalInput").ap()
    w1 = nc.dram_tensor("w1", [D, FF1], F32, kind="ExternalInput").ap()
    w2 = nc.dram_tensor("w2", [FF1, D], F32, kind="ExternalInput").ap()
    bq_c = nc.dram_tensor("bq_c", [D, 1], F32, kind="ExternalInput").ap()
    bo_c = nc.dram_tensor("bo_c", [D, 1], F32, kind="ExternalInput").ap()
    b1_c = nc.dram_tensor("b1_c", [FF1, 1], F32, kind="ExternalInput").ap()
    b2_c = nc.dram_tensor("b2_c", [D, 1], F32, kind="ExternalInput").ap()
    bk_row = nc.dram_tensor("bk_row", [1, D], F32, kind="ExternalInput").ap()
    bv_row = nc.dram_tensor("bv_row", [1, D], F32, kind="ExternalInput").ap()
    g1_row = nc.dram_tensor("g1_row", [1, D], F32, kind="ExternalInput").ap()
    be1_row = nc.dram_tensor("be1_row", [1, D], F32, kind="ExternalInput").ap()
    g2_row = nc.dram_tensor("g2_row", [1, D], F32, kind="ExternalInput").ap()
    be2_row = nc.dram_tensor("be2_row", [1, D], F32, kind="ExternalInput").ap()
    x4out = nc.dram_tensor("x4out", [M, D], F32, kind="ExternalOutput").ap()
    ctx = ExitStack()
    emit_l1_attn(nc, ctx, x2full, x2own, wq, wk, wv, wo, w1, w2,
                 bq_c, bo_c, b1_c, b2_c, bk_row, bv_row,
                 g1_row, be1_row, g2_row, be2_row, x4out)
    ctx.close()
    return nc


def l1attn_in_maps(inp, x2):
    col = lambda k, n: np.ascontiguousarray(np.asarray(inp[k]).reshape(n, 1))
    row = lambda k: np.ascontiguousarray(np.asarray(inp[k]).reshape(1, D))
    shared = {
        "wq": np.ascontiguousarray(np.asarray(inp["L1_attn_Wq"])),
        "wk": np.ascontiguousarray(np.asarray(inp["L1_attn_Wk"])),
        "wv": np.ascontiguousarray(np.asarray(inp["L1_attn_Wv"])),
        "wo": np.ascontiguousarray(np.asarray(inp["L1_attn_Wo"])),
        "w1": np.ascontiguousarray(np.asarray(inp["L1_attn_W1"])),
        "w2": np.ascontiguousarray(np.asarray(inp["L1_attn_W2"])),
        "bq_c": col("L1_attn_bq", D), "bo_c": col("L1_attn_bo", D),
        "b1_c": col("L1_attn_b1", FF1), "b2_c": col("L1_attn_b2", D),
        "bk_row": row("L1_attn_bk"), "bv_row": row("L1_attn_bv"),
        "g1_row": row("L1_attn_g1"), "be1_row": row("L1_attn_beta1"),
        "g2_row": row("L1_attn_g2"), "be2_row": row("L1_attn_beta2"),
    }
    in_maps = []
    for c in range(8):
        b, j = divmod(c, 4)
        m = dict(shared)
        m["x2full"] = np.ascontiguousarray(x2[b])
        m["x2own"] = np.ascontiguousarray(x2[b, j * M:(j + 1) * M])
        in_maps.append(m)
    return in_maps


def run_l1attn(inp, x2):
    nc = build_nc_l1attn()
    res = run_bass_kernel_spmd(nc, l1attn_in_maps(inp, x2), core_ids=list(range(8)))
    x4 = np.stack([res.results[c]["x4out"] for c in range(8)])
    return x4.reshape(B, S, D)


def build_nc_l1moe():
    nc = bass.Bass("TRN2", target_bir_lowering=False)
    x_in = nc.dram_tensor("xd_in", [M, D], F32, kind="ExternalInput").ap()
    wr = nc.dram_tensor("wr", [D, E], F32, kind="ExternalInput").ap()
    br_row = nc.dram_tensor("br_row", [1, E], F32, kind="ExternalInput").ap()
    w1e = [nc.dram_tensor(f"w1e{e}", [D, FMOE], F32, kind="ExternalInput").ap()
           for e in range(E)]
    w2e = [nc.dram_tensor(f"w2e{e}", [FMOE, D], F32, kind="ExternalInput").ap()
           for e in range(E)]
    b1_c = nc.dram_tensor("b1m_c", [E * FMOE, 1], F32, kind="ExternalInput").ap()
    b2_c = nc.dram_tensor("b2m_c", [E * D, 1], F32, kind="ExternalInput").ap()
    out = nc.dram_tensor("out", [M, D], F32, kind="ExternalOutput").ap()
    ctx = ExitStack()
    emit_l1_moe(nc, ctx, x_in, wr, br_row, w1e, b1_c, w2e, b2_c, out)
    ctx.close()
    return nc


def l1moe_in_maps(inp, x4):
    shared = {
        "wr": np.ascontiguousarray(np.asarray(inp["L1_ffn_wr"])),
        "br_row": np.ascontiguousarray(np.asarray(inp["L1_ffn_br"]).reshape(1, E)),
        "b1m_c": np.ascontiguousarray(np.asarray(inp["L1_ffn_b1"]).reshape(E * FMOE, 1)),
        "b2m_c": np.ascontiguousarray(np.asarray(inp["L1_ffn_b2"]).reshape(E * D, 1)),
    }
    for e in range(E):
        shared[f"w1e{e}"] = np.ascontiguousarray(np.asarray(inp["L1_ffn_w1"])[e])
        shared[f"w2e{e}"] = np.ascontiguousarray(np.asarray(inp["L1_ffn_w2"])[e])
    in_maps = []
    for c in range(8):
        b, j = divmod(c, 4)
        m = dict(shared)
        m["xd_in"] = np.ascontiguousarray(x4[b, j * M:(j + 1) * M])
        in_maps.append(m)
    return in_maps


def run_l1moe(inp, x4):
    nc = build_nc_l1moe()
    res = run_bass_kernel_spmd(nc, l1moe_in_maps(inp, x4), core_ids=list(range(8)))
    o = np.stack([res.results[c]["out"] for c in range(8)])
    return o.reshape(B, S, D)


def l0attn_in_maps(inp):
    x = np.asarray(inp["L0_attn_queries"], dtype=np.float32)
    keys = np.ascontiguousarray(np.asarray(inp["L0_attn_keys"])[0])
    values = np.ascontiguousarray(np.asarray(inp["L0_attn_values"])[0])
    in_maps = []
    for c in range(8):
        b, j = divmod(c, 4)
        o0 = 256 * j
        in_maps.append({
            "xfull": np.ascontiguousarray(x[b]),
            "xown": np.ascontiguousarray(x[b, j * M:(j + 1) * M]),
            "keys": keys,
            "values": values,
            "wqT": np.ascontiguousarray(np.asarray(inp["L0_attn_Wq_w"])[o0:o0 + 256, :].T),
            "wkT": np.ascontiguousarray(np.asarray(inp["L0_attn_Wk_w"])[o0:o0 + 256, :].T),
            "wvT": np.ascontiguousarray(np.asarray(inp["L0_attn_Wv_w"])[o0:o0 + 256, :].T),
            "bq": np.ascontiguousarray(np.asarray(inp["L0_attn_Wq_b"])[o0:o0 + 256].reshape(256, 1)),
            "bk": np.ascontiguousarray(np.asarray(inp["L0_attn_Wk_b"])[o0:o0 + 256].reshape(256, 1)),
            "bv": np.ascontiguousarray(np.asarray(inp["L0_attn_Wv_b"])[o0:o0 + 256].reshape(256, 1)),
        })
    return in_maps


def run_l0attn(inp):
    nc = build_nc_l0attn()
    res = run_bass_kernel_spmd(nc, l0attn_in_maps(inp), core_ids=list(range(8)))
    x1 = np.stack([res.results[c]["x1out"] for c in range(8)])
    return x1.reshape(B, S, D)


def build_nc_A():
    nc = bass.Bass("TRN2", target_bir_lowering=False)
    xfull = nc.dram_tensor("xfull", [S, D], F32, kind="ExternalInput").ap()
    xown = nc.dram_tensor("xown", [M, D], F32, kind="ExternalInput").ap()
    keys = nc.dram_tensor("keys", [KS, D], F32, kind="ExternalInput").ap()
    values = nc.dram_tensor("values", [KS, D], F32, kind="ExternalInput").ap()
    wqT = nc.dram_tensor("wqT", [D, 256], F32, kind="ExternalInput").ap()
    wkT = nc.dram_tensor("wkT", [D, 256], F32, kind="ExternalInput").ap()
    wvT = nc.dram_tensor("wvT", [D, 256], F32, kind="ExternalInput").ap()
    bq = nc.dram_tensor("bq", [256, 1], F32, kind="ExternalInput").ap()
    bk = nc.dram_tensor("bk", [256, 1], F32, kind="ExternalInput").ap()
    bv = nc.dram_tensor("bv", [256, 1], F32, kind="ExternalInput").ap()
    wgT = nc.dram_tensor("wgT", [D, FF0], I32, kind="ExternalInput").ap()
    wuT = nc.dram_tensor("wuT", [D, FF0], I32, kind="ExternalInput").ap()
    wdT = nc.dram_tensor("wdT", [FF0, D], I32, kind="ExternalInput").ap()
    x1mid = nc.dram_tensor("x1mid", [M, D], F32, kind="Internal").ap()
    x2out = nc.dram_tensor("x2out", [M, D], F32, kind="ExternalOutput").ap()
    ctx = ExitStack()
    emit_l0_attn(nc, ctx, xfull, xown, keys, values, wqT, wkT, wvT, bq, bk, bv,
                 x1mid)
    ctx.close()
    ctx2 = ExitStack()
    emit_l0_ffn(nc, ctx2, x1mid, wgT, wuT, wdT, x2out)
    ctx2.close()
    return nc


def run_A(inp):
    nc = build_nc_A()
    ffnw = {
        "wgT": np.ascontiguousarray(np.asarray(inp["L0_ffn_w_gate"]).T),
        "wuT": np.ascontiguousarray(np.asarray(inp["L0_ffn_w_up"]).T),
        "wdT": np.ascontiguousarray(np.asarray(inp["L0_ffn_w_down"]).T),
    }
    im = l0attn_in_maps(inp)
    for m in im:
        m.update(ffnw)
    res = run_bass_kernel_spmd(nc, im, core_ids=list(range(8)))
    x2 = np.stack([res.results[c]["x2out"] for c in range(8)])
    return x2.reshape(B, S, D)


def build_nc_B():
    nc = bass.Bass("TRN2", target_bir_lowering=False)
    x2full = nc.dram_tensor("x2full", [S, D], F32, kind="ExternalInput").ap()
    x2own = nc.dram_tensor("x2own", [M, D], F32, kind="ExternalInput").ap()
    wq = nc.dram_tensor("wq", [D, D], F32, kind="ExternalInput").ap()
    wk = nc.dram_tensor("wk", [D, D], F32, kind="ExternalInput").ap()
    wv = nc.dram_tensor("wv", [D, D], F32, kind="ExternalInput").ap()
    wo = nc.dram_tensor("wo", [D, D], F32, kind="ExternalInput").ap()
    w1 = nc.dram_tensor("w1", [D, FF1], F32, kind="ExternalInput").ap()
    w2 = nc.dram_tensor("w2", [FF1, D], F32, kind="ExternalInput").ap()
    bq_c = nc.dram_tensor("bq_c", [D, 1], F32, kind="ExternalInput").ap()
    bo_c = nc.dram_tensor("bo_c", [D, 1], F32, kind="ExternalInput").ap()
    b1_c = nc.dram_tensor("b1_c", [FF1, 1], F32, kind="ExternalInput").ap()
    b2_c = nc.dram_tensor("b2_c", [D, 1], F32, kind="ExternalInput").ap()
    bk_row = nc.dram_tensor("bk_row", [1, D], F32, kind="ExternalInput").ap()
    bv_row = nc.dram_tensor("bv_row", [1, D], F32, kind="ExternalInput").ap()
    g1_row = nc.dram_tensor("g1_row", [1, D], F32, kind="ExternalInput").ap()
    be1_row = nc.dram_tensor("be1_row", [1, D], F32, kind="ExternalInput").ap()
    g2_row = nc.dram_tensor("g2_row", [1, D], F32, kind="ExternalInput").ap()
    be2_row = nc.dram_tensor("be2_row", [1, D], F32, kind="ExternalInput").ap()
    wr = nc.dram_tensor("wr", [D, E], F32, kind="ExternalInput").ap()
    br_row = nc.dram_tensor("br_row", [1, E], F32, kind="ExternalInput").ap()
    w1e = [nc.dram_tensor(f"w1e{e}", [D, FMOE], F32, kind="ExternalInput").ap()
           for e in range(E)]
    w2e = [nc.dram_tensor(f"w2e{e}", [FMOE, D], F32, kind="ExternalInput").ap()
           for e in range(E)]
    b1m_c = nc.dram_tensor("b1m_c", [E * FMOE, 1], F32, kind="ExternalInput").ap()
    b2m_c = nc.dram_tensor("b2m_c", [E * D, 1], F32, kind="ExternalInput").ap()
    x4mid = nc.dram_tensor("x4mid", [M, D], F32, kind="Internal").ap()
    out = nc.dram_tensor("out", [M, D], F32, kind="ExternalOutput").ap()
    ctx = ExitStack()
    emit_l1_attn(nc, ctx, x2full, x2own, wq, wk, wv, wo, w1, w2,
                 bq_c, bo_c, b1_c, b2_c, bk_row, bv_row,
                 g1_row, be1_row, g2_row, be2_row, x4mid)
    ctx.close()
    ctx2 = ExitStack()
    emit_l1_moe(nc, ctx2, x4mid, wr, br_row, w1e, b1m_c, w2e, b2m_c, out)
    ctx2.close()
    return nc


def run_B(inp, x2):
    nc = build_nc_B()
    moew = {
        "wr": np.ascontiguousarray(np.asarray(inp["L1_ffn_wr"])),
        "br_row": np.ascontiguousarray(np.asarray(inp["L1_ffn_br"]).reshape(1, E)),
        "b1m_c": np.ascontiguousarray(np.asarray(inp["L1_ffn_b1"]).reshape(E * FMOE, 1)),
        "b2m_c": np.ascontiguousarray(np.asarray(inp["L1_ffn_b2"]).reshape(E * D, 1)),
    }
    for e in range(E):
        moew[f"w1e{e}"] = np.ascontiguousarray(np.asarray(inp["L1_ffn_w1"])[e])
        moew[f"w2e{e}"] = np.ascontiguousarray(np.asarray(inp["L1_ffn_w2"])[e])
    im = l1attn_in_maps(inp, x2)
    for m in im:
        m.update(moew)
    res = run_bass_kernel_spmd(nc, im, core_ids=list(range(8)))
    o = np.stack([res.results[c]["out"] for c in range(8)])
    return o.reshape(B, S, D)


def kernel(**inputs):
    x2 = run_A(inputs)
    out = run_B(inputs, x2)
    return out
